# version 30
# baseline (speedup 1.0000x reference)
import contextlib
import numpy as np
import concourse.bass as bass
import concourse.mybir as mybir
from concourse.bass_utils import run_bass_kernel_spmd

F32 = mybir.dt.float32
BF16 = mybir.dt.bfloat16
AF = mybir.ActivationFunctionType
OP = mybir.AluOpType
AX = mybir.AxisListType

MAGIC = 12582912.0
EPS = 1e-5
NCORES = 8
D = 1024
T = 2048
NS = 64
DC = 8
IC = 32
TT = 512
NTILE = T // TT
NBLK = TT // 128
WSIZE = float(4096 * 1024)

ENGS = ("sync", "scalar", "vector", "gpsimd", "tensor")
NSLOT = 8


class Sched:

    def __init__(self):
        self.ops = []
        self.lastw = {}
        self.readers = {}
        self.fence_deps = ()

    def add(self, eng, fn, *args, reads=(), writes=(), dma=False, cc=False,
            **kwargs):
        idx = len(self.ops)
        deps = set(self.fence_deps)
        for r in reads:
            j = self.lastw.get(r)
            if j is not None:
                deps.add(j)
        for r in writes:
            j = self.lastw.get(r)
            if j is not None:
                deps.add(j)
            deps.update(self.readers.get(r, ()))
        deps.discard(idx)
        try:
            sh = args[0].shape
            osz = 1
            for d in sh[1:]:
                osz *= d
        except Exception:
            osz = 1 << 30
        self.ops.append(dict(idx=idx, eng=eng, fn=fn, args=args, kwargs=kwargs,
                             dma=dma, cc=cc, deps=deps, marked=False,
                             rd=frozenset(reads), wr=frozenset(writes),
                             osz=osz))
        for r in reads:
            self.readers.setdefault(r, []).append(idx)
        for r in writes:
            self.lastw[r] = idx
            self.readers[r] = []
        return idx

    def fence(self):
        last = {}
        for op in self.ops:
            last[op["eng"]] = op["idx"]
        self.fence_deps = tuple(last.values())

    def finalize(self):
        ops = self.ops
        for op in ops:
            for d in op["deps"]:
                dep = ops[d]
                if dep["dma"] or dep["eng"] != op["eng"]:
                    dep["marked"] = True
        pos = {}
        cnt = {e: 0 for e in ENGS}
        for op in ops:
            pos[op["idx"]] = cnt[op["eng"]]
            cnt[op["eng"]] += 1
        for op in ops:
            if op["eng"] != "vector" or op["dma"]:
                continue
            op["vsync"] = []
            for d in op["deps"]:
                dep = ops[d]
                if dep["eng"] != "vector" or dep["dma"]:
                    continue
                dist = pos[op["idx"]] - pos[d]
                raw_waw = bool(dep["wr"] & (op["rd"] | op["wr"]))
                if not raw_waw:
                    continue
                if dep["osz"] < 256 and dist <= 12:
                    dep["marked"] = True
                    op["vsync"].append(d)
                elif dist == 1:
                    op["nop_before"] = True
        ccnt = {e: 0 for e in ENGS}
        dcnt = {e: 0 for e in ENGS}
        ncc = 0
        for op in ops:
            if op["dma"]:
                i = dcnt[op["eng"]]
                dcnt[op["eng"]] += 1
                op["dslot"] = i % NSLOT
                op["dval"] = (i // NSLOT + 1) * 16
            elif op["cc"]:
                ncc += 1
                op["cval"] = ncc
            elif op["marked"]:
                ccnt[op["eng"]] += 1
                op["cval"] = ccnt[op["eng"]]
        self.final_c = ccnt
        self.final_d = {}
        for e in ENGS:
            n = dcnt[e]
            for sl in range(NSLOT):
                gens = (n - sl + NSLOT - 1) // NSLOT
                if gens > 0:
                    self.final_d[(e, sl)] = gens * 16
        hw = {e: {} for e in ENGS}
        for op in ops:
            waits = {}
            if op["dma"] and op["dval"] > 16:
                waits[("d", op["eng"], op["dslot"])] = op["dval"] - 16
            for d in sorted(op["deps"]):
                dep = ops[d]
                if dep["dma"]:
                    sem, val = ("d", dep["eng"], dep["dslot"]), dep["dval"]
                elif dep["cc"]:
                    sem, val = ("cc",), dep["cval"]
                elif dep["eng"] != op["eng"]:
                    sem, val = ("c", dep["eng"]), dep["cval"]
                elif d in op.get("vsync", ()):
                    sem, val = ("c", dep["eng"]), dep["cval"]
                else:
                    continue
                if val > waits.get(sem, 0):
                    waits[sem] = val
            h = hw[op["eng"]]
            wl = []
            for sem, val in waits.items():
                if val > h.get(sem, 0):
                    wl.append((sem, val))
                    h[sem] = val
            op["waits"] = wl

    def replay(self, eng_name, eng, sems):
        for op in self.ops:
            if op["eng"] != eng_name:
                continue
            for sem, val in op["waits"]:
                eng.wait_ge(sems[sem], val)
            if op.get("nop_before"):
                eng.engine_nop()
            ins = getattr(eng, op["fn"])(*op["args"], **op["kwargs"])
            if op["dma"]:
                ins.then_inc(sems[("d", eng_name, op["dslot"])], 16)
            elif op["cc"]:
                ins.then_inc(sems[("cc",)], op["cval"] -
                             (op["cval"] - 1))
            elif op["marked"]:
                ins.then_inc(sems[("c", eng_name)], 1)


def build_program(debug=False, cc_local=False):
    nc = bass.Bass("TRN2", target_bir_lowering=False,
                   detect_race_conditions=False)

    dx = nc.dram_tensor("x_in", [DC, 128, T], F32, kind="ExternalInput").ap()
    dlar = nc.dram_tensor("lar", [NS, 1], F32, kind="ExternalInput").ap()
    dlai = nc.dram_tensor("lai", [NS, 1], F32, kind="ExternalInput").ap()
    dbw = nc.dram_tensor("bwT", [DC, 128, 128], F32, kind="ExternalInput").ap()
    dc1 = nc.dram_tensor("c1T", [NS, D], F32, kind="ExternalInput").ap()
    dc2 = nc.dram_tensor("c2T", [NS, D], F32, kind="ExternalInput").ap()
    ddw = nc.dram_tensor("dwT", [DC, 128, D], F32, kind="ExternalInput").ap()
    dwg = nc.dram_tensor("wgT", [16, 128, DC, 256], F32,
                         kind="ExternalInput").ap()
    dwu = nc.dram_tensor("wuT", [16, 128, DC, 256], F32,
                         kind="ExternalInput").ap()
    dwd = nc.dram_tensor("wdT", [IC, 128, D], F32, kind="ExternalInput").ap()
    dident = nc.dram_tensor("ident", [128, 128], F32, kind="ExternalInput").ap()
    dy = nc.dram_tensor("y", [DC, 128, T], F32, kind="ExternalOutput").ap()
    stg = nc.dram_tensor("tern_g", [16, 128, DC, 256], BF16,
                         kind="Internal").ap()
    stu = nc.dram_tensor("tern_u", [16, 128, DC, 256], BF16,
                         kind="Internal").ap()
    std_ = nc.dram_tensor("tern_d", [IC, 128, D], BF16, kind="Internal").ap()
    tmine = nc.dram_tensor("tern_mine", [3, 2, 128, 2048], BF16,
                           kind="Internal").ap()
    tall = nc.dram_tensor("tern_all", [24, 2, 128, 2048], BF16,
                          kind="Internal", addr_space="Shared").ap()
    drows = [nc.dram_tensor(f"drow{i}", [1, TT], F32, kind="Internal")
             for i in range(8)]
    drowb = [nc.dram_tensor(f"drowb{i}", [1, TT], BF16, kind="Internal")
             for i in range(2)]
    dsh = nc.dram_tensor("wshard", [6, 128, 2048], F32,
                         kind="ExternalInput").ap()
    dccin = nc.dram_tensor("ccin", [1, 4], F32, kind="Internal")
    dccout = nc.dram_tensor("ccout", [1, 4], F32, kind="Internal",
                        addr_space="Shared")

    dbg = {}
    if debug:
        for nm, shape, dt in [
            ("d_xq1", [128, DC, T], BF16), ("d_ucx", [NS, 2 * T], F32),
            ("d_Ut", [NS, 2 * T], F32), ("d_rbuf", [NS, T], F32),
            ("d_h", [NS, 2 * T], BF16), ("d_x1", [128, DC, T], F32),
            ("d_apar", [NS, 16], F32), ("d_mg", [128, 3], F32),
            ("d_s", [128, 3], F32), ("d_q1", [128, DC, TT], BF16),
            ("d_gates", [128, IC, TT], F32), ("d_guq", [128, IC, TT], BF16),
            ("d_rows", [128, NBLK], F32),
        ]:
            dbg[nm] = nc.dram_tensor(nm, shape, dt, kind="ExternalOutput").ap()

    S = Sched()
    stack = contextlib.ExitStack()

    def sb(name, shape, dtype, st=None):
        return (st or stack).enter_context(nc.sbuf_tensor(name, shape, dtype))

    x = sb("x", [128, DC, T], F32)
    ident = sb("identity", [128, 128], F32)
    ones_f = sb("ones_f", [128, 1], F32)
    ones_b = sb("ones_b", [128, 1], BF16)
    ones_r = sb("ones_r", [1, 128], F32)
    mg128 = sb("mg128", [128, 3], F32)
    s128 = sb("s128", [128, 3], F32)
    wsum = sb("wsum", [128, 34], F32)
    wrow = sb("wrow", [1, 8], F32)
    sqf = [sb(f"sqf{i}", [128, TT], BF16) for i in range(2)]
    xb = [sb(f"xb{i}", [128, TT], BF16) for i in range(2)]
    rows = {k: sb("row_" + k, [1, TT], F32) for k in "ABCDEF"}

    ps = [stack.enter_context(nc.psum_tensor(f"ps{i}", [128, TT], F32))
          for i in range(8)]
    sems = {}
    for e in ENGS:
        sems[("c", e)] = stack.enter_context(nc.semaphore(f"c_{e}"))
        for sl in range(NSLOT):
            sems[("d", e, sl)] = stack.enter_context(
                nc.semaphore(f"d_{e}_{sl}"))
    sems[("cc",)] = stack.enter_context(nc.semaphore("ccsem"))

    def rne_round(out_ap, in_ap, reads, writes):
        S.add("vector", "tensor_scalar", out_ap, in_ap, MAGIC, MAGIC,
              OP.add, OP.subtract, reads=reads, writes=writes)

    def ln_stats(tile):
        t0 = tile * TT
        for ci in range(DC):
            b = ci % 2
            S.add("scalar", "copy", xb[b][:], x[:, ci, t0:t0 + TT],
                  reads=[("x", ci, tile)], writes=[("xb", b)])
            S.add("scalar", "activation", sqf[b][:], x[:, ci, t0:t0 + TT],
                  AF.Square, reads=[("x", ci, tile)], writes=[("sqf", b)])
            S.add("tensor", "matmul", ps[6][0:1, 0:TT], ones_b[:], xb[b][:],
                  start=(ci == 0), stop=(ci == DC - 1),
                  reads=[("xb", b), "ones_b"], writes=[("ps", 6)])
            S.add("tensor", "matmul", ps[5][0:1, 0:TT], ones_b[:], sqf[b][:],
                  start=(ci == 0), stop=(ci == DC - 1),
                  reads=[("sqf", b), "ones_b"], writes=[("ps", 5)])

    def rstd_rows(bank_unused, mean_row, rstd_row, tmp_row):
        S.add("vector", "tensor_scalar", rows[mean_row][:],
              ps[6][0:1, 0:TT], 1.0 / D, None, OP.mult,
              reads=[("ps", 6)], writes=[("row", mean_row)])
        S.add("vector", "tensor_scalar", rows[tmp_row][:],
              ps[5][0:1, 0:TT], 1.0 / D, None, OP.mult,
              reads=[("ps", 5)], writes=[("row", tmp_row)])
        S.add("vector", "tensor_tensor", rows[rstd_row][:], rows[mean_row][:],
              rows[mean_row][:], OP.mult, reads=[("row", mean_row)],
              writes=[("row", rstd_row)])
        S.add("vector", "tensor_tensor", rows[tmp_row][:], rows[tmp_row][:],
              rows[rstd_row][:], OP.subtract,
              reads=[("row", tmp_row), ("row", rstd_row)],
              writes=[("row", tmp_row)])
        S.add("vector", "tensor_scalar", rows[tmp_row][:], rows[tmp_row][:],
              EPS, None, OP.add, reads=[("row", tmp_row)],
              writes=[("row", tmp_row)])
        S.add("scalar", "activation", rows[rstd_row][:], rows[tmp_row][:],
              AF.Ln, reads=[("row", tmp_row)], writes=[("row", rstd_row)])
        S.add("scalar", "activation", rows[rstd_row][:], rows[rstd_row][:],
              AF.Exp, scale=-0.5, reads=[("row", rstd_row)],
              writes=[("row", rstd_row)])

    bcast_slot = [0]

    def bcast_row(row_key, dst, dst_key):
        sl = bcast_slot[0] % 8
        bcast_slot[0] += 1
        S.add("gpsimd", "dma_start", drows[sl].ap(), rows[row_key][:],
              dma=True, reads=[("row", row_key)], writes=[("drow", sl)])
        S.add("gpsimd", "dma_start", dst[:],
              drows[sl].broadcast_to([128, TT])[:, :], dma=True,
              reads=[("drow", sl)], writes=[dst_key])

    def tm_to_row(tm_ap, tm_keys, row_key, psn=7):
        for blk in range(NBLK):
            S.add("tensor", "matmul", ps[psn][0:1, blk * 128:(blk + 1) * 128],
                  tm_ap[:, blk:blk + 1], ident[:], start=True, stop=True,
                  reads=tm_keys + ["ident"], writes=[("ps", psn)])
        S.add("vector", "tensor_copy", rows[row_key][:], ps[psn][0:1, 0:TT],
              reads=[("ps", psn)], writes=[("row", row_key)])

    def amax_reduce(src_ap, src_keys, out_row, psn=7):
        for blk in range(NBLK):
            S.add("tensor", "transpose",
                  ps[psn][0:128, blk * 128:(blk + 1) * 128],
                  src_ap[:, blk * 128:(blk + 1) * 128], ident[:],
                  reads=src_keys + ["ident"], writes=[("ps", psn)])
        for blk in range(NBLK):
            S.add("vector", "tensor_reduce", amax_tm[:, blk:blk + 1],
                  ps[psn][0:128, blk * 128:(blk + 1) * 128], AX.X, OP.max,
                  reads=[("ps", psn)], writes=["amax_tm"])
        tm_to_row(amax_tm[:], ["amax_tm"], out_row, psn)

    def recip_row(dst, src, tmp):
        S.add("scalar", "activation", rows[tmp][:], rows[src][:], AF.Ln,
              reads=[("row", src)], writes=[("row", tmp)])
        S.add("scalar", "activation", rows[dst][:], rows[tmp][:], AF.Exp,
              scale=-1.0, reads=[("row", tmp)], writes=[("row", dst)])
        S.add("vector", "tensor_tensor", rows[tmp][:], rows[src][:],
              rows[dst][:], OP.mult, reads=[("row", src), ("row", dst)],
              writes=[("row", tmp)])
        S.add("vector", "tensor_scalar", rows[tmp][:], rows[tmp][:], -1.0, 2.0,
              OP.mult, OP.add, reads=[("row", tmp)], writes=[("row", tmp)])
        S.add("vector", "tensor_tensor", rows[dst][:], rows[dst][:],
              rows[tmp][:], OP.mult, reads=[("row", dst), ("row", tmp)],
              writes=[("row", dst)])

    S.add("sync", "dma_start", ident[:], dident, dma=True, writes=["ident"])
    S.add("vector", "memset", ones_f[:], 1.0, writes=["ones_f"])
    S.add("vector", "memset", ones_b[:], 1.0, writes=["ones_b"])
    S.add("vector", "memset", ones_r[:], 1.0, writes=["ones_r"])
    for ci in range(DC):
        S.add("gpsimd", "dma_start", x[:, ci, :], dx[ci], dma=True,
              writes=[("x", ci, tt) for tt in range(NTILE)])

    ast = contextlib.ExitStack()
    crow = sb("crow", [1, 4], F32, ast)
    dstg = [sb(f"dstg{i}", [128, D], F32, ast) for i in range(1)]
    bca = {k: sb("bca_" + k, [128, TT], BF16, ast) for k in ("P", "Q")}
    rowbf = sb("rowbf", [1, TT], BF16, ast)
    bw_bf = sb("bw_bf", [128, DC, 128], BF16, ast)
    c1_bf = sb("c1_bf", [NS, D], BF16, ast)
    c2_bf = sb("c2_bf", [NS, D], BF16, ast)
    dw_bf = sb("dw_bf", [128, DC, D], BF16, ast)
    apar = sb("apar", [NS, 16], F32, ast)
    xq1 = sb("xq1", [128, DC, T], BF16, ast)
    ucx = sb("ucx", [NS, 2 * T], F32, ast)
    Ut = sb("Ut", [NS, 2 * T], F32, ast)
    bp = sb("bp", [NS, T], F32, ast)
    rbuf = sb("rbuf", [NS, T], F32, ast)
    h_bf = sb("h_bf", [NS, 2 * T], BF16, ast)

    S.add("sync", "dma_start", apar[:, 0:1], dlar, dma=True,
          writes=[("apar", 0)])
    S.add("sync", "dma_start", apar[:, 1:2], dlai, dma=True,
          writes=[("apar", 1)])
    for ci in range(DC):
        b = 0
        S.add("sync", "dma_start", dstg[b][:, 0:128], dbw[ci], dma=True,
              writes=[("dstg", b)])
        S.add("scalar", "copy", bw_bf[:, ci, :], dstg[b][:, 0:128],
              reads=[("dstg", b)], writes=["bw_bf"])
    for src, dst, key in ((dc1, c1_bf, "c1_bf"), (dc2, c2_bf, "c2_bf")):
        S.add("sync", "dma_start", dstg[0][0:NS, :], src, dma=True,
              writes=[("dstg", 0)])
        S.add("scalar", "copy", dst[:], dstg[0][0:NS, :],
              reads=[("dstg", 0)], writes=[key])
    for ci in range(DC):
        S.add("sync", "dma_start", dstg[0][:], ddw[ci], dma=True,
              writes=[("dstg", 0)])
        S.add("scalar", "copy", dw_bf[:, ci, :], dstg[0][:],
              reads=[("dstg", 0)], writes=[("dw_bf", ci)])

    def _shard_stats():
        for i in range(12):
            c, h = i // 2, i % 2
            S.add("scalar", "dma_start", dstg[0][:, 0:1024],
                  dsh[c, :, h * 1024:(h + 1) * 1024], dma=True,
                  writes=[("dstg", 0)])
            S.add("scalar", "activation", dstg[0][:, 0:1024], dstg[0][:, 0:1024],
                  AF.Abs, accum_out=wsum[:, i:i + 1],
                  reads=[("dstg", 0)], writes=[("dstg", 0), ("wsum", i)])
        for m in range(3):
            S.add("vector", "tensor_reduce", wsum[:, 16 + m:17 + m],
                  wsum[:, 4 * m:4 * m + 4], AX.X, OP.add,
                  reads=[("wsum", i) for i in range(4 * m, 4 * m + 4)],
                  writes=[("wsum", "t", m)])
            S.add("tensor", "matmul", ps[7][0:1, m:m + 1],
                  wsum[:, 16 + m:17 + m], ones_f[:], start=True, stop=True,
                  reads=[("wsum", "t", m), "ones_f"], writes=[("ps", 7)])
        S.add("vector", "memset", crow[:], 0.0, writes=["crow"])
        S.add("vector", "tensor_copy", crow[:, 0:3], ps[7][0:1, 0:3],
              reads=[("ps", 7), "crow"], writes=["crow"])
        S.add("scalar", "dma_start", dccin.ap(), crow[:], dma=True,
              reads=["crow"], writes=["dccin"])
        if cc_local:
            S.add("gpsimd", "dma_start", dccout.ap(), dccin.ap(), dma=True,
                  reads=["dccin"], writes=["dccout"])
        else:
            S.add("gpsimd", "collective_compute", "AllReduce", OP.add,
                  [[0, 1, 2, 3, 4, 5, 6, 7]], [dccin.ap()], [dccout.ap()],
                  cc=True, reads=["dccin"], writes=["dccout"])
    def _scale_chain():
        S.add("scalar", "dma_start", crow[:], dccout.ap(), dma=True,
              reads=["dccout"], writes=["crow"])
        for m in range(3):
            S.add("vector", "tensor_scalar", wrow[0:1, 0:1], crow[:, m:m + 1],
                  1.0 / WSIZE, 1e-5, OP.mult, OP.max,
                  reads=["crow"], writes=[("wrow", 0)])
            S.add("scalar", "activation", wrow[0:1, 1:2], wrow[0:1, 0:1], AF.Ln,
                  reads=[("wrow", 0)], writes=[("wrow", 1)])
            S.add("scalar", "activation", wrow[0:1, 2:3], wrow[0:1, 1:2],
                  AF.Exp, scale=-1.0, reads=[("wrow", 1)], writes=[("wrow", 2)])
            S.add("vector", "tensor_tensor", wrow[0:1, 3:4], wrow[0:1, 0:1],
                  wrow[0:1, 2:3], OP.mult, reads=[("wrow", 0), ("wrow", 2)],
                  writes=[("wrow", 3)])
            S.add("vector", "tensor_scalar", wrow[0:1, 3:4], wrow[0:1, 3:4],
                  -1.0, 2.0, OP.mult, OP.add, reads=[("wrow", 3)],
                  writes=[("wrow", 3)])
            S.add("vector", "tensor_tensor", wrow[0:1, 2:3], wrow[0:1, 2:3],
                  wrow[0:1, 3:4], OP.mult, reads=[("wrow", 2), ("wrow", 3)],
                  writes=[("wrow", 2)])
            S.add("tensor", "matmul", ps[7][:, 0:1], ones_r[:], wrow[0:1, 0:1],
                  start=True, stop=True, reads=[("wrow", 0), "ones_r"],
                  writes=[("ps", 7)])
            S.add("tensor", "matmul", ps[7][:, 1:2], ones_r[:], wrow[0:1, 2:3],
                  start=True, stop=True, reads=[("wrow", 2), "ones_r"],
                  writes=[("ps", 7)])
            S.add("vector", "tensor_copy", mg128[:, m:m + 1], ps[7][:, 0:1],
                  reads=[("ps", 7)], writes=[("mg128", m)])
            S.add("vector", "tensor_copy", s128[:, m:m + 1], ps[7][:, 1:2],
                  reads=[("ps", 7)], writes=[("s128", m)])


    def a_params():
        A = apar

        def k(j):
            return ("apar", j)

        def ts(dst, src, s1, s2, o1, o2=None, extra=()):
            S.add("vector", "tensor_scalar", A[:, dst:dst + 1],
                  A[:, src:src + 1], s1, s2, o1, *([o2] if o2 else []),
                  reads=[k(src)] + list(extra), writes=[k(dst)])

        def tt(dst, a1, a2, op):
            S.add("vector", "tensor_tensor", A[:, dst:dst + 1],
                  A[:, a1:a1 + 1], A[:, a2:a2 + 1], op,
                  reads=[k(a1), k(a2)], writes=[k(dst)])

        def act(dst, src, func, **kw):
            S.add("scalar", "activation", A[:, dst:dst + 1],
                  A[:, src:src + 1], func, reads=[k(src)], writes=[k(dst)],
                  **kw)

        act(2, 0, AF.Exp)
        ts(3, 2, -0.5, None, OP.mult)
        ts(4, 1, 0.5, None, OP.mult)
        ts(7, 3, -1.0, 1.0, OP.mult, OP.add)
        tt(5, 7, 7, OP.mult)
        tt(7, 4, 4, OP.mult)
        tt(5, 5, 7, OP.add)
        act(14, 5, AF.Ln)
        act(6, 14, AF.Exp, scale=-1.0)
        tt(14, 5, 6, OP.mult)
        ts(14, 14, -1.0, 2.0, OP.mult, OP.add)
        tt(6, 6, 14, OP.mult)
        tt(8, 3, 3, OP.mult)
        tt(7, 4, 4, OP.mult)
        tt(8, 8, 7, OP.add)
        ts(8, 8, -1.0, 1.0, OP.mult, OP.add)
        tt(8, 8, 6, OP.mult)
        ts(9, 4, 2.0, None, OP.mult)
        tt(9, 9, 6, OP.mult)
        tt(10, 8, 8, OP.mult)
        tt(7, 9, 9, OP.mult)
        tt(10, 10, 7, OP.add)
        ts(7, 10, 1e-38, None, OP.max)
        act(14, 7, AF.Ln)
        act(11, 14, AF.Exp, scale=0.5)
        ts(7, 11, 1e-30, None, OP.max)
        act(14, 7, AF.Ln)
        act(7, 14, AF.Exp, scale=-1.0)
        tt(12, 8, 7, OP.mult)
        tt(13, 9, 7, OP.mult)
        ts(14, 11, -1e-30, 0.0, OP.add, OP.max)
        S.add("scalar", "sign", A[:, 14:15], A[:, 14:15],
              reads=[k(14)], writes=[k(14)])
        ts(7, 14, -1.0, 1.0, OP.mult, OP.add)
        tt(12, 12, 14, OP.mult)
        tt(12, 12, 7, OP.add)
        tt(13, 13, 14, OP.mult)
        tt(7, 12, 12, OP.mult)
        tt(14, 13, 13, OP.mult)
        tt(7, 7, 14, OP.add)
        ts(7, 7, -0.5, 1.5, OP.mult, OP.add)
        tt(12, 12, 7, OP.mult)
        tt(13, 13, 7, OP.mult)

    def u_table():
        A = apar
        S.add("vector", "memset", Ut[0:NS, 0:2 * T], 0.0, writes=[("Ut", "z")])
        S.add("vector", "memset", Ut[0:NS, 0:1], 1.0,
              reads=[("Ut", "z")], writes=[("Ut", 0)])
        S.add("vector", "memset", rbuf[:], 1.0, writes=["rbuf"])
        S.add("vector", "tensor_scalar", rbuf[:], rbuf[:], apar[:, 11:12],
              None, OP.mult, reads=["rbuf", ("apar", 11)], writes=["rbuf"])
        L = 1
        while L < T:
            upto = min(L, T - L)
            seg = int(np.log2(L)) + 1
            rdU = [("Ut", j) for j in range(seg)] + [("Ut", "z")]
            kP = [("apar", 12), ("apar", 13)]
            S.add("vector", "tensor_scalar", bp[0:NS, 0:upto],
                  Ut[0:NS, 0:upto], A[:, 12:13], None, OP.mult,
                  reads=rdU + kP, writes=[("bp", "t")])
            S.add("vector", "tensor_scalar", A[:, 15:16], A[:, 13:14], -1.0,
                  None, OP.mult, reads=[("apar", 13)], writes=[("apar", 15)])
            S.add("vector", "scalar_tensor_tensor", Ut[0:NS, L:L + upto],
                  Ut[0:NS, T:T + upto], A[:, 15:16], bp[0:NS, 0:upto],
                  OP.mult, OP.add,
                  reads=rdU + kP + [("apar", 15), ("bp", "t")],
                  writes=[("Ut", seg)])
            S.add("vector", "tensor_scalar", bp[0:NS, 0:upto],
                  Ut[0:NS, T:T + upto], A[:, 12:13], None, OP.mult,
                  reads=rdU + kP, writes=[("bp", "t")])
            S.add("vector", "scalar_tensor_tensor",
                  Ut[0:NS, T + L:T + L + upto], Ut[0:NS, 0:upto],
                  A[:, 13:14], bp[0:NS, 0:upto], OP.mult, OP.add,
                  reads=rdU + kP + [("bp", "t")], writes=[("Ut", seg)])
            S.add("vector", "tensor_tensor", A[:, 14:15], A[:, 12:13],
                  A[:, 12:13], OP.mult, reads=[("apar", 12)],
                  writes=[("apar", 14)])
            S.add("vector", "tensor_tensor", A[:, 15:16], A[:, 13:14],
                  A[:, 13:14], OP.mult, reads=[("apar", 13)],
                  writes=[("apar", 15)])
            S.add("vector", "tensor_tensor", A[:, 13:14], A[:, 12:13],
                  A[:, 13:14], OP.mult, reads=[("apar", 12), ("apar", 13)],
                  writes=[("apar", 13)])
            S.add("vector", "tensor_scalar", A[:, 13:14], A[:, 13:14], 2.0,
                  None, OP.mult, reads=[("apar", 13)], writes=[("apar", 13)])
            S.add("vector", "tensor_tensor", A[:, 12:13], A[:, 14:15],
                  A[:, 15:16], OP.subtract,
                  reads=[("apar", 14), ("apar", 15)], writes=[("apar", 12)])
            L *= 2

    UKEYS = [("Ut", j) for j in range(12)] + [("Ut", "z")]

    def ln1_tile(tile):
        t0 = tile * TT
        ln_stats(tile)
        rstd_rows(6, "A", "B", "C")
        S.add("vector", "tensor_tensor", rows["C"][:], rows["A"][:],
              rows["B"][:], OP.mult, reads=[("row", "A"), ("row", "B")],
              writes=[("row", "C")])
        for rk, bk, sl in (("B", "P", 0), ("C", "Q", 1)):
            S.add("vector", "tensor_copy", rowbf[:], rows[rk][:],
                  reads=[("row", rk)], writes=["rowbf"])
            S.add("gpsimd", "dma_start", drowb[sl].ap(), rowbf[:], dma=True,
                  reads=["rowbf"], writes=[("drowb", sl)])
            S.add("gpsimd", "dma_start", bca[bk][:],
                  drowb[sl].broadcast_to([128, TT])[:, :], dma=True,
                  reads=[("drowb", sl)], writes=[("bca", bk)])
        for ci in range(DC):
            b = ci % 2
            S.add("vector", "tensor_tensor", sqf[b][:],
                  x[:, ci, t0:t0 + TT], bca["P"][:], OP.mult,
                  reads=[("x", ci, tile), ("bca", "P")], writes=[("sqf", b)])
            S.add("vector", "tensor_tensor", xq1[:, ci, t0:t0 + TT],
                  sqf[b][:], bca["Q"][:], OP.subtract,
                  reads=[("sqf", b), ("bca", "Q")],
                  writes=[("xq1", ci, tile)])

    def bu_tile(tile):
        t0 = tile * TT
        for half in (0, 1):
            pi = half
            for ci in range(DC):
                S.add("tensor", "matmul", ps[pi][0:NS, 0:TT],
                      bw_bf[:, ci, half * NS:(half + 1) * NS],
                      xq1[:, ci, t0:t0 + TT],
                      start=(ci == 0), stop=(ci == DC - 1),
                      reads=[("xq1", ci, tile), "bw_bf"], writes=[("ps", pi)])
            S.add("vector", "tensor_copy",
                  ucx[0:NS, half * T + t0: half * T + t0 + TT],
                  ps[pi][0:NS, 0:TT], reads=[("ps", pi)],
                  writes=[("ucx", half, tile)])

    def rotate_scan():
        UK = UKEYS
        BK = [("ucx", hh, tt) for hh in (0, 1) for tt in range(NTILE)]
        S.add("vector", "tensor_tensor", h_bf[0:NS, 0:T], Ut[0:NS, T:2 * T],
              ucx[0:NS, 0:T], OP.mult, reads=UK + BK, writes=[("h", "tmpA")])
        S.add("vector", "tensor_tensor", ucx[0:NS, 0:T], ucx[0:NS, 0:T],
              Ut[0:NS, 0:T], OP.mult, reads=UK + BK, writes=[("ucx", "re1")])
        S.add("vector", "tensor_tensor", bp[0:NS, 0:T], Ut[0:NS, T:2 * T],
              ucx[0:NS, T:2 * T], OP.mult, reads=UK + BK, writes=[("bp", "t")])
        S.add("vector", "tensor_tensor", ucx[0:NS, 0:T], ucx[0:NS, 0:T],
              bp[0:NS, 0:T], OP.add,
              reads=[("ucx", "re1"), ("bp", "t")], writes=[("ucx", "re")])
        S.add("vector", "tensor_tensor", ucx[0:NS, T:2 * T],
              ucx[0:NS, T:2 * T], Ut[0:NS, 0:T], OP.mult,
              reads=UK + BK, writes=[("ucx", "im1")])
        S.add("vector", "tensor_tensor", ucx[0:NS, T:2 * T],
              ucx[0:NS, T:2 * T], h_bf[0:NS, 0:T], OP.subtract,
              reads=[("ucx", "im1"), ("h", "tmpA")], writes=[("ucx", "im")])
        S.add("vector", "tensor_tensor_scan", ucx[0:NS, 0:T], rbuf[:],
              ucx[0:NS, 0:T], 0.0, OP.mult, OP.add,
              reads=["rbuf", ("ucx", "re")], writes=[("ucx", "gre")])
        S.add("vector", "tensor_tensor_scan", ucx[0:NS, T:2 * T], rbuf[:],
              ucx[0:NS, T:2 * T], 0.0, OP.mult, OP.add,
              reads=["rbuf", ("ucx", "im")], writes=[("ucx", "gim")])
        S.add("vector", "tensor_tensor", bp[0:NS, 0:T], Ut[0:NS, T:2 * T],
              ucx[0:NS, T:2 * T], OP.mult, reads=UK + [("ucx", "gim")],
              writes=[("bp", "t2")])
        S.add("vector", "tensor_tensor", h_bf[0:NS, T:2 * T],
              Ut[0:NS, T:2 * T], ucx[0:NS, 0:T], OP.mult,
              reads=UK + [("ucx", "gre")], writes=[("h", "tmpX")])
        S.add("vector", "tensor_tensor", ucx[0:NS, 0:T], ucx[0:NS, 0:T],
              Ut[0:NS, 0:T], OP.mult, reads=UK + [("ucx", "gre")],
              writes=[("ucx", "p1")])
        S.add("vector", "tensor_tensor", ucx[0:NS, T:2 * T],
              ucx[0:NS, T:2 * T], Ut[0:NS, 0:T], OP.mult,
              reads=UK + [("ucx", "gim")], writes=[("ucx", "p3")])
        S.add("vector", "tensor_tensor", h_bf[0:NS, 0:T], ucx[0:NS, 0:T],
              bp[0:NS, 0:T], OP.subtract,
              reads=[("ucx", "p1"), ("bp", "t2")], writes=[("h", 0)])
        S.add("vector", "tensor_tensor", h_bf[0:NS, T:2 * T],
              ucx[0:NS, T:2 * T], h_bf[0:NS, T:2 * T], OP.add,
              reads=[("ucx", "p3"), ("h", "tmpX")], writes=[("h", 1)])

    def attn_out_tile(tile):
        t0 = tile * TT
        for oc in range(DC):
            pi = 2 + (oc % 4)
            S.add("tensor", "matmul", ps[pi][:, 0:TT],
                  c1_bf[0:NS, oc * 128:(oc + 1) * 128],
                  h_bf[0:NS, t0:t0 + TT], start=True, stop=False,
                  reads=[("h", 0), "c1_bf"], writes=[("ps", pi)])
            S.add("tensor", "matmul", ps[pi][:, 0:TT],
                  c2_bf[0:NS, oc * 128:(oc + 1) * 128],
                  h_bf[0:NS, T + t0:T + t0 + TT], start=False, stop=False,
                  reads=[("h", 1), "c2_bf"], writes=[("ps", pi)])
            for ci in range(DC):
                S.add("tensor", "matmul", ps[pi][:, 0:TT],
                      dw_bf[:, ci, oc * 128:(oc + 1) * 128],
                      xq1[:, ci, t0:t0 + TT],
                      start=False, stop=(ci == DC - 1),
                      reads=[("xq1", ci, tile), ("dw_bf", ci)],
                      writes=[("ps", pi)])
            S.add("vector", "tensor_tensor", x[:, oc, t0:t0 + TT],
                  x[:, oc, t0:t0 + TT], ps[pi][:, 0:TT], OP.add,
                  reads=[("x", oc, tile), ("ps", pi)],
                  writes=[("x", oc, tile)])

    def dump(nm, src_ap, read_keys):
        if debug:
            S.add("gpsimd", "dma_start", dbg[nm], src_ap, dma=True,
                  reads=read_keys, writes=[("dbg", nm)])

    _shard_stats()
    a_params()
    u_table()
    ln1_tile(0)
    ln1_tile(1)
    bu_tile(0)
    ln1_tile(2)
    bu_tile(1)
    _scale_chain()
    ln1_tile(3)
    bu_tile(2)
    bu_tile(3)
    rotate_scan()
    for tile in range(NTILE):
        attn_out_tile(tile)
    dump("d_xq1", xq1[:], [("xq1", ci, tt) for ci in range(DC)
                           for tt in range(NTILE)])
    dump("d_Ut", Ut[:], UKEYS)
    dump("d_rbuf", rbuf[:], ["rbuf"])
    dump("d_h", h_bf[:], [("h", 0), ("h", 1)])
    dump("d_apar", apar[:], [("apar", j) for j in range(16)])
    dump("d_x1", x[:], [("x", ci, tt) for ci in range(DC)
                        for tt in range(NTILE)])

    S.fence()
    ast.close()

    wst = contextlib.ExitStack()
    wrawQ = [sb(f"wrawQ{i}", [128, 4096], F32, wst) for i in range(3)]
    wtern = [sb(f"wtern{i}", [128, 4096], BF16, wst) for i in range(3)]
    negM2 = sb("negM2", [128, 1], F32, wst)
    S.add("vector", "memset", negM2[:], -MAGIC, writes=["negM2"])

    for k in range(3):
        S.add("sync", "dma_start", wrawQ[k][:], dsh[2 * k:2 * k + 2],
              dma=True, writes=[("wrawQ", k)])
        S.add("vector", "tensor_scalar", wrawQ[k][:], wrawQ[k][:],
              s128[:, k:k + 1], MAGIC, OP.mult, OP.add,
              reads=[("wrawQ", k), ("s128", k)], writes=[("wrawQ", k)])
        S.add("vector", "tensor_scalar", wrawQ[k][:], wrawQ[k][:],
              MAGIC + 1.0, MAGIC - 1.0, OP.min, OP.max,
              reads=[("wrawQ", k)], writes=[("wrawQ", k)])
        S.add("scalar", "activation", wtern[k][:], wrawQ[k][:],
              AF.Identity, bias=negM2[:, 0:1],
              reads=[("wrawQ", k), "negM2"], writes=[("wtern", k)])
        S.add("gpsimd", "dma_start", tmine[k], wtern[k][:], dma=True,
              reads=[("wtern", k)], writes=[("mine", k)])
    if cc_local:
        S.add("gpsimd", "dma_start", tall[0:3], tmine, dma=True,
              reads=[("mine", k) for k in range(3)], writes=["ternall"])
    else:
        S.add("gpsimd", "collective_compute", "AllGather", OP.bypass,
              [[0, 1, 2, 3, 4, 5, 6, 7]], [tmine], [tall],
              cc=True, reads=[("mine", k) for k in range(3)],
              writes=["ternall"])

    if debug:
        d_tall = nc.dram_tensor("d_tall", [24, 2, 128, 2048], BF16,
                                kind="ExternalOutput").ap()
        S.add("gpsimd", "dma_start", d_tall, tall, dma=True,
              reads=["ternall"])

    S.fence()
    wst.close()

    fst = contextlib.ExitStack()
    wgs = [sb(f"wgs{i}", [128, DC, 256], BF16, fst) for i in range(2)]
    wus = [sb(f"wus{i}", [128, DC, 256], BF16, fst) for i in range(2)]
    wdk = [sb(f"wdk{i}", [128, D], BF16, fst) for i in range(2)]
    q1 = sb("q1", [128, DC, TT], BF16, fst)
    gates = sb("gates", [128, IC, TT], F32, fst)
    guq = gates.bitcast(BF16)
    bc = {k: sb("bc_" + k, [128, TT], F32, fst) for k in ("P", "Q", "S")}
    gsc = [sb(f"gsc{i}", [128, TT], F32, fst) for i in range(2)]
    absb = [sb(f"absb{i}", [128, TT], F32, fst) for i in range(1)]
    cmax = sb("cmax", [128, TT], F32, fst)
    amax_tm = sb("amax_tm", [128, NBLK], F32, fst)
    def s1a(tile):
        t0 = tile * TT
        ln_stats(tile)
        rstd_rows(6, "A", "B", "C")
        bcast_row("A", bc["P"], ("bc", "P"))
        S.add("vector", "memset", cmax[:], 0.0, writes=["cmax"])
        for ci in range(DC):
            b = ci % 2
            S.add("vector", "tensor_tensor", gsc[b][:], x[:, ci, t0:t0 + TT],
                  bc["P"][:], OP.subtract,
                  reads=[("x", ci, tile), ("bc", "P")], writes=[("gsc", b)])
            S.add("scalar", "activation", absb[0][:], gsc[b][:], AF.Abs,
                  reads=[("gsc", b)], writes=[("absb", 0)])
            S.add("vector", "tensor_tensor", cmax[:], cmax[:], absb[0][:],
                  OP.max, reads=["cmax", ("absb", 0)], writes=["cmax"])
        amax_reduce(cmax[:], ["cmax"], "C")

    def s1b1(tile):
        S.add("vector", "tensor_tensor", rows["C"][:], rows["C"][:],
              rows["B"][:], OP.mult, reads=[("row", "C"), ("row", "B")],
              writes=[("row", "C")])
        S.add("vector", "tensor_scalar", rows["C"][:], rows["C"][:], 1e-5,
              None, OP.max, reads=[("row", "C")], writes=[("row", "C")])
        recip_row("D", "C", "F")
        S.add("vector", "tensor_scalar", rows["D"][:], rows["D"][:], 127.0,
              None, OP.mult, reads=[("row", "D")], writes=[("row", "D")])
        S.add("vector", "tensor_tensor", rows["D"][:], rows["D"][:],
              rows["B"][:], OP.mult, reads=[("row", "D"), ("row", "B")],
              writes=[("row", "D")])
        S.add("vector", "tensor_scalar", rows["A"][:], rows["C"][:],
              1.0 / 127.0, None, OP.mult, reads=[("row", "C")],
              writes=[("row", "A")])

    def s1b2(tile):
        t0 = tile * TT
        bcast_row("D", bc["Q"], ("bc", "Q"))
        for ci in range(DC):
            b = ci % 2
            S.add("vector", "tensor_tensor", gsc[b][:], x[:, ci, t0:t0 + TT],
                  bc["P"][:], OP.subtract,
                  reads=[("x", ci, tile), ("bc", "P")], writes=[("gsc", b)])
            S.add("vector", "tensor_tensor", gsc[b][:], gsc[b][:],
                  bc["Q"][:], OP.mult,
                  reads=[("gsc", b), ("bc", "Q")], writes=[("gsc", b)])
            rne_round(q1[:, ci, :], gsc[b][:], reads=[("gsc", b)],
                      writes=[("q1", ci)])
        S.add("vector", "tensor_scalar", rows["F"][:], rows["A"][:],
              mg128[0:1, 0:1], None, OP.mult,
              reads=[("row", "A"), ("mg128", 0)], writes=[("row", "F")])
        bcast_row("F", bc["P"], ("bc", "P"))
        S.add("vector", "tensor_scalar", rows["C"][:], rows["A"][:],
              mg128[0:1, 1:2], None, OP.mult,
              reads=[("row", "A"), ("mg128", 1)], writes=[("row", "C")])

    def s2(tile):
        S.add("vector", "memset", cmax[:], 0.0, writes=["cmax"])
        for oc in range(IC):
            ob, j = oc // 2, oc % 2
            bb = ob % 2
            if j == 0:
                S.add("sync", "dma_start", wgs[bb][:],
                      tall[3 * (ob // 2), ob % 2], dma=True,
                      reads=["ternall"], writes=[("wgs", bb)])
                S.add("sync", "dma_start", wus[bb][:],
                      tall[3 * (ob // 2) + 1, ob % 2], dma=True,
                      reads=["ternall"], writes=[("wus", bb)])
            pg = oc % 3
            pu = 3 + oc % 3
            for ci in range(DC):
                S.add("tensor", "matmul", ps[pg][:, 0:TT],
                      wgs[bb][:, ci, j * 128:(j + 1) * 128], q1[:, ci, :],
                      start=(ci == 0), stop=(ci == DC - 1),
                      reads=[("wgs", bb), ("q1", ci)], writes=[("ps", pg)])
            S.add("vector", "tensor_tensor", gsc[oc % 2][:], ps[pg][:, 0:TT],
                  bc["P"][:], OP.mult, reads=[("ps", pg), ("bc", "P")],
                  writes=[("gsc", oc % 2)])
            S.add("scalar", "activation", gates[:, oc, :], gsc[oc % 2][:],
                  AF.Silu, reads=[("gsc", oc % 2)],
                  writes=[("gates", oc), ("guq", oc)])
            for ci in range(DC):
                S.add("tensor", "matmul", ps[pu][:, 0:TT],
                      wus[bb][:, ci, j * 128:(j + 1) * 128], q1[:, ci, :],
                      start=(ci == 0), stop=(ci == DC - 1),
                      reads=[("wus", bb), ("q1", ci)], writes=[("ps", pu)])
            S.add("vector", "tensor_tensor", gates[:, oc, :],
                  gates[:, oc, :], ps[pu][:, 0:TT], OP.mult,
                  reads=[("gates", oc), ("ps", pu)],
                  writes=[("gates", oc), ("guq", oc)])
            S.add("scalar", "activation", absb[0][:], gates[:, oc, :],
                  AF.Abs, reads=[("gates", oc)], writes=[("absb", 0)])
            S.add("vector", "tensor_tensor", cmax[:], cmax[:],
                  absb[0][:], OP.max,
                  reads=["cmax", ("absb", 0)], writes=["cmax"])

    def s3(tile):
        amax_reduce(cmax[:], ["cmax"], "B")
        S.add("vector", "tensor_tensor", rows["E"][:], rows["C"][:],
              rows["B"][:], OP.mult, reads=[("row", "C"), ("row", "B")],
              writes=[("row", "E")])
        S.add("vector", "tensor_scalar", rows["E"][:], rows["E"][:], 1e-5,
              None, OP.max, reads=[("row", "E")], writes=[("row", "E")])
        recip_row("D", "E", "F")
        S.add("vector", "tensor_scalar", rows["D"][:], rows["D"][:], 127.0,
              None, OP.mult, reads=[("row", "D")], writes=[("row", "D")])
        S.add("vector", "tensor_tensor", rows["D"][:], rows["D"][:],
              rows["C"][:], OP.mult, reads=[("row", "D"), ("row", "C")],
              writes=[("row", "D")])
        S.add("vector", "tensor_scalar", rows["E"][:], rows["E"][:],
              mg128[0:1, 2:3], 1.0 / 127.0, OP.mult, OP.mult,
              reads=[("row", "E"), ("mg128", 2)], writes=[("row", "E")])
        bcast_row("D", bc["Q"], ("bc", "Q"))
        bcast_row("E", bc["S"], ("bc", "S"))

    def s4(tile):
        for oc in range(IC):
            b = oc % 2
            S.add("vector", "tensor_tensor", gsc[b][:], gates[:, oc, :],
                  bc["Q"][:], OP.mult, reads=[("gates", oc), ("bc", "Q")],
                  writes=[("gsc", b)])
            rne_round(guq[:, oc, 0:TT], gsc[b][:], reads=[("gsc", b)],
                      writes=[("guq", oc), ("gates", oc)])

    def s5(tile):
        t0 = tile * TT
        for kc in range(IC):
            bb = kc % 2
            hd = (kc % 2) * 1024
            S.add("sync", "dma_start", wdk[bb][:],
                  tall[3 * (kc // 4) + 2, (kc % 4) // 2, :, hd:hd + 1024],
                  dma=True, reads=["ternall"], writes=[("wdk", bb)])
            for od in range(DC):
                S.add("tensor", "matmul", ps[od][:, 0:TT],
                      wdk[bb][:, od * 128:(od + 1) * 128], guq[:, kc, 0:TT],
                      start=(kc == 0), stop=(kc == IC - 1),
                      reads=[("wdk", bb), ("guq", kc)], writes=[("ps", od)])
        for od in range(DC):
            S.add("vector", "tensor_tensor", gsc[od % 2][:],
                  ps[od][:, 0:TT], bc["S"][:], OP.mult,
                  reads=[("ps", od), ("bc", "S")], writes=[("gsc", od % 2)])
            S.add("vector", "tensor_tensor", x[:, od, t0:t0 + TT],
                  x[:, od, t0:t0 + TT], gsc[od % 2][:], OP.add,
                  reads=[("x", od, tile), ("gsc", od % 2)],
                  writes=[("x", od, tile)])
            S.add("gpsimd", "dma_start", dy[od, :, t0:t0 + TT],
                  x[:, od, t0:t0 + TT], dma=True,
                  reads=[("x", od, tile)], writes=[("y", od, tile)])

    s1a(0)
    s1b1(0)
    s1b2(0)
    for tile in range(NTILE):
        s2(tile)
        s3(tile)
        if tile + 1 < NTILE:
            s1a(tile + 1)
        s4(tile)
        if tile + 1 < NTILE:
            s1b1(tile + 1)
            s1b2(tile + 1)
        s5(tile)
        if debug and tile == 0:
            dump("d_q1", q1[:], [("q1", ci) for ci in range(DC)])
            dump("d_gates", gates[:], [("gates", oc) for oc in range(IC)])
            dump("d_guq", guq[:, :, 0:TT], [("guq", oc) for oc in range(IC)])
            dump("d_rows", rows["A"][:], [("row", "A")])

    S.finalize()

    with nc.Block() as blk:
        @blk.sync
        def _(eng):
            S.replay("sync", eng, sems)
            for key, val in S.final_d.items():
                eng.wait_ge(sems[("d",) + key], val)
            for e in ENGS:
                if S.final_c[e]:
                    eng.wait_ge(sems[("c", e)], S.final_c[e])

        @blk.scalar
        def _(eng):
            S.replay("scalar", eng, sems)

        @blk.vector
        def _(eng):
            S.replay("vector", eng, sems)

        @blk.gpsimd
        def _(eng):
            S.replay("gpsimd", eng, sems)

        @blk.tensor
        def _(eng):
            S.replay("tensor", eng, sems)

    return nc, stack, fst


def kernel(L0_attn_u, L0_attn_log_A_real, L0_attn_log_A_imag, L0_attn_B_w,
           L0_attn_C_w, L0_attn_D_w, L0_ffn_w_gate, L0_ffn_w_up,
           L0_ffn_w_down):
    u = np.ascontiguousarray(L0_attn_u, dtype=np.float32)
    lar = np.ascontiguousarray(L0_attn_log_A_real, np.float32).reshape(NS, 1)
    lai = np.ascontiguousarray(L0_attn_log_A_imag, np.float32).reshape(NS, 1)
    bwT = np.ascontiguousarray(np.asarray(L0_attn_B_w, np.float32).T
                               ).reshape(DC, 128, 128)
    c1T = np.ascontiguousarray(np.asarray(L0_attn_C_w, np.float32)[:, :NS].T)
    c2T = np.ascontiguousarray(np.asarray(L0_attn_C_w, np.float32)[:, NS:].T)
    dwT = np.ascontiguousarray(np.asarray(L0_attn_D_w, np.float32).T
                               ).reshape(DC, 128, D)
    def _wg_layout(w):
        wt = np.asarray(w, np.float32).T.reshape(DC, 128, 16, 256)
        return np.ascontiguousarray(wt.transpose(2, 1, 0, 3))

    wgT = _wg_layout(L0_ffn_w_gate)
    wuT = _wg_layout(L0_ffn_w_up)
    wdT = np.ascontiguousarray(np.asarray(L0_ffn_w_down, np.float32).T
                               ).reshape(IC, 128, D)
    ident = np.eye(128, dtype=np.float32)

    nc, stack, fst = build_program()
    in_maps = []
    for c in range(NCORES):
        xc = np.ascontiguousarray(u[c].T).reshape(DC, 128, T)
        shard = np.stack([
            wgT[2 * c].reshape(128, 2048),
            wgT[2 * c + 1].reshape(128, 2048),
            wuT[2 * c].reshape(128, 2048),
            wuT[2 * c + 1].reshape(128, 2048),
            np.concatenate([wdT[4 * c], wdT[4 * c + 1]], axis=1),
            np.concatenate([wdT[4 * c + 2], wdT[4 * c + 3]], axis=1),
        ])
        in_maps.append({
            "x_in": xc, "lar": lar, "lai": lai, "bwT": bwT, "c1T": c1T,
            "c2T": c2T, "dwT": dwT, "wgT": wgT, "wuT": wuT, "wdT": wdT,
            "ident": ident, "wshard": np.ascontiguousarray(shard),
        })
    res = run_bass_kernel_spmd(nc, in_maps, core_ids=list(range(NCORES)))
    fst.close()
    stack.close()
    out = np.empty((NCORES, T, D), np.float32)
    for c in range(NCORES):
        yc = np.asarray(res.results[c]["y"])
        out[c] = yc.reshape(D, T).T
    return out


# version 31
# speedup vs baseline: 1.0539x; 1.0539x over previous
import contextlib
import numpy as np
import concourse.bass as bass
import concourse.mybir as mybir
from concourse.bass_utils import run_bass_kernel_spmd

F32 = mybir.dt.float32
BF16 = mybir.dt.bfloat16
AF = mybir.ActivationFunctionType
OP = mybir.AluOpType
AX = mybir.AxisListType

MAGIC = 12582912.0
EPS = 1e-5
NCORES = 8
D = 1024
T = 2048
NS = 64
DC = 8
IC = 32
TT = 512
NTILE = T // TT
NBLK = TT // 128
WSIZE = float(4096 * 1024)

ENGS = ("sync", "scalar", "vector", "gpsimd", "tensor")
NSLOT = 8


class Sched:

    def __init__(self):
        self.ops = []
        self.lastw = {}
        self.readers = {}
        self.fence_deps = ()

    def add(self, eng, fn, *args, reads=(), writes=(), dma=False, cc=False,
            **kwargs):
        idx = len(self.ops)
        deps = set(self.fence_deps)
        for r in reads:
            j = self.lastw.get(r)
            if j is not None:
                deps.add(j)
        for r in writes:
            j = self.lastw.get(r)
            if j is not None:
                deps.add(j)
            deps.update(self.readers.get(r, ()))
        deps.discard(idx)
        try:
            sh = args[0].shape
            osz = 1
            for d in sh[1:]:
                osz *= d
        except Exception:
            osz = 1 << 30
        self.ops.append(dict(idx=idx, eng=eng, fn=fn, args=args, kwargs=kwargs,
                             dma=dma, cc=cc, deps=deps, marked=False,
                             rd=frozenset(reads), wr=frozenset(writes),
                             osz=osz))
        for r in reads:
            self.readers.setdefault(r, []).append(idx)
        for r in writes:
            self.lastw[r] = idx
            self.readers[r] = []
        return idx

    def fence(self):
        last = {}
        for op in self.ops:
            last[op["eng"]] = op["idx"]
        self.fence_deps = tuple(last.values())

    def finalize(self):
        ops = self.ops
        for op in ops:
            for d in op["deps"]:
                dep = ops[d]
                if dep["dma"] or dep["eng"] != op["eng"]:
                    dep["marked"] = True
        pos = {}
        cnt = {e: 0 for e in ENGS}
        for op in ops:
            pos[op["idx"]] = cnt[op["eng"]]
            cnt[op["eng"]] += 1
        for op in ops:
            if op["eng"] != "vector" or op["dma"]:
                continue
            op["vsync"] = []
            for d in op["deps"]:
                dep = ops[d]
                if dep["eng"] != "vector" or dep["dma"]:
                    continue
                dist = pos[op["idx"]] - pos[d]
                raw_waw = bool(dep["wr"] & (op["rd"] | op["wr"]))
                if not raw_waw:
                    continue
                if dep["osz"] < 256 and dist <= 12:
                    dep["marked"] = True
                    op["vsync"].append(d)
                elif dist == 1:
                    op["nop_before"] = True
        ccnt = {e: 0 for e in ENGS}
        dcnt = {e: 0 for e in ENGS}
        ncc = 0
        for op in ops:
            if op["dma"]:
                i = dcnt[op["eng"]]
                dcnt[op["eng"]] += 1
                op["dslot"] = i % NSLOT
                op["dval"] = (i // NSLOT + 1) * 16
            elif op["cc"]:
                ncc += 1
                op["cval"] = ncc
            elif op["marked"]:
                ccnt[op["eng"]] += 1
                op["cval"] = ccnt[op["eng"]]
        self.final_c = ccnt
        self.final_d = {}
        for e in ENGS:
            n = dcnt[e]
            for sl in range(NSLOT):
                gens = (n - sl + NSLOT - 1) // NSLOT
                if gens > 0:
                    self.final_d[(e, sl)] = gens * 16
        hw = {e: {} for e in ENGS}
        for op in ops:
            waits = {}
            if op["dma"] and op["dval"] > 16:
                waits[("d", op["eng"], op["dslot"])] = op["dval"] - 16
            for d in sorted(op["deps"]):
                dep = ops[d]
                if dep["dma"]:
                    sem, val = ("d", dep["eng"], dep["dslot"]), dep["dval"]
                elif dep["cc"]:
                    sem, val = ("cc",), dep["cval"]
                elif dep["eng"] != op["eng"]:
                    sem, val = ("c", dep["eng"]), dep["cval"]
                elif d in op.get("vsync", ()):
                    sem, val = ("c", dep["eng"]), dep["cval"]
                else:
                    continue
                if val > waits.get(sem, 0):
                    waits[sem] = val
            h = hw[op["eng"]]
            wl = []
            for sem, val in waits.items():
                if val > h.get(sem, 0):
                    wl.append((sem, val))
                    h[sem] = val
            op["waits"] = wl

    def replay(self, eng_name, eng, sems):
        for op in self.ops:
            if op["eng"] != eng_name:
                continue
            for sem, val in op["waits"]:
                eng.wait_ge(sems[sem], val)
            if op.get("nop_before"):
                eng.engine_nop()
            ins = getattr(eng, op["fn"])(*op["args"], **op["kwargs"])
            if op["dma"]:
                ins.then_inc(sems[("d", eng_name, op["dslot"])], 16)
            elif op["cc"]:
                ins.then_inc(sems[("cc",)], op["cval"] -
                             (op["cval"] - 1))
            elif op["marked"]:
                ins.then_inc(sems[("c", eng_name)], 1)


def build_program(debug=False, cc_local=False):
    nc = bass.Bass("TRN2", target_bir_lowering=False,
                   detect_race_conditions=False)

    dx = nc.dram_tensor("x_in", [DC, 128, T], F32, kind="ExternalInput").ap()
    dlar = nc.dram_tensor("lar", [NS, 1], F32, kind="ExternalInput").ap()
    dlai = nc.dram_tensor("lai", [NS, 1], F32, kind="ExternalInput").ap()
    dbw = nc.dram_tensor("bwT", [DC, 128, 128], F32, kind="ExternalInput").ap()
    dc1 = nc.dram_tensor("c1T", [NS, D], F32, kind="ExternalInput").ap()
    dc2 = nc.dram_tensor("c2T", [NS, D], F32, kind="ExternalInput").ap()
    ddw = nc.dram_tensor("dwT", [DC, 128, D], F32, kind="ExternalInput").ap()
    dwg = nc.dram_tensor("wgT", [16, 128, DC, 256], F32,
                         kind="ExternalInput").ap()
    dwu = nc.dram_tensor("wuT", [16, 128, DC, 256], F32,
                         kind="ExternalInput").ap()
    dwd = nc.dram_tensor("wdT", [IC, 128, D], F32, kind="ExternalInput").ap()
    dident = nc.dram_tensor("ident", [128, 128], F32, kind="ExternalInput").ap()
    dy = nc.dram_tensor("y", [DC, 128, T], F32, kind="ExternalOutput").ap()
    stg = nc.dram_tensor("tern_g", [16, 128, DC, 256], BF16,
                         kind="Internal").ap()
    stu = nc.dram_tensor("tern_u", [16, 128, DC, 256], BF16,
                         kind="Internal").ap()
    std_ = nc.dram_tensor("tern_d", [IC, 128, D], BF16, kind="Internal").ap()
    tmine = nc.dram_tensor("tern_mine", [3, 2, 128, 2048], BF16,
                           kind="Internal").ap()
    tall = nc.dram_tensor("tern_all", [24, 2, 128, 2048], BF16,
                          kind="Internal", addr_space="Shared").ap()
    drows = [nc.dram_tensor(f"drow{i}", [1, TT], F32, kind="Internal")
             for i in range(8)]
    drowb = [nc.dram_tensor(f"drowb{i}", [1, TT], BF16, kind="Internal")
             for i in range(2)]
    dsh = nc.dram_tensor("wshard", [6, 128, 2048], F32,
                         kind="ExternalInput").ap()
    dccin = nc.dram_tensor("ccin", [1, 4], F32, kind="Internal")
    dccout = nc.dram_tensor("ccout", [1, 4], F32, kind="Internal")

    dbg = {}
    if debug:
        for nm, shape, dt in [
            ("d_xq1", [128, DC, T], BF16), ("d_ucx", [NS, 2 * T], F32),
            ("d_Ut", [NS, 2 * T], F32), ("d_rbuf", [NS, T], F32),
            ("d_h", [NS, 2 * T], BF16), ("d_x1", [128, DC, T], F32),
            ("d_apar", [NS, 16], F32), ("d_mg", [128, 3], F32),
            ("d_s", [128, 3], F32), ("d_q1", [128, DC, TT], BF16),
            ("d_gates", [128, IC, TT], F32), ("d_guq", [128, IC, TT], BF16),
            ("d_rows", [128, NBLK], F32),
        ]:
            dbg[nm] = nc.dram_tensor(nm, shape, dt, kind="ExternalOutput").ap()

    S = Sched()
    stack = contextlib.ExitStack()

    def sb(name, shape, dtype, st=None):
        return (st or stack).enter_context(nc.sbuf_tensor(name, shape, dtype))

    x = sb("x", [128, DC, T], F32)
    ident = sb("identity", [128, 128], F32)
    ones_f = sb("ones_f", [128, 1], F32)
    ones_b = sb("ones_b", [128, 1], BF16)
    ones_r = sb("ones_r", [1, 128], F32)
    mg128 = sb("mg128", [128, 3], F32)
    s128 = sb("s128", [128, 3], F32)
    wsum = sb("wsum", [128, 34], F32)
    wrow = sb("wrow", [1, 8], F32)
    sqf = [sb(f"sqf{i}", [128, TT], BF16) for i in range(2)]
    xb = [sb(f"xb{i}", [128, TT], BF16) for i in range(2)]
    rows = {k: sb("row_" + k, [1, TT], F32) for k in "ABCDEF"}

    ps = [stack.enter_context(nc.psum_tensor(f"ps{i}", [128, TT], F32))
          for i in range(8)]
    sems = {}
    for e in ENGS:
        sems[("c", e)] = stack.enter_context(nc.semaphore(f"c_{e}"))
        for sl in range(NSLOT):
            sems[("d", e, sl)] = stack.enter_context(
                nc.semaphore(f"d_{e}_{sl}"))
    sems[("cc",)] = stack.enter_context(nc.semaphore("ccsem"))

    def rne_round(out_ap, in_ap, reads, writes):
        S.add("vector", "tensor_scalar", out_ap, in_ap, MAGIC, MAGIC,
              OP.add, OP.subtract, reads=reads, writes=writes)

    def ln_stats(tile):
        t0 = tile * TT
        for ci in range(DC):
            b = ci % 2
            S.add("scalar", "copy", xb[b][:], x[:, ci, t0:t0 + TT],
                  reads=[("x", ci, tile)], writes=[("xb", b)])
            S.add("scalar", "activation", sqf[b][:], x[:, ci, t0:t0 + TT],
                  AF.Square, reads=[("x", ci, tile)], writes=[("sqf", b)])
            S.add("tensor", "matmul", ps[6][0:1, 0:TT], ones_b[:], xb[b][:],
                  start=(ci == 0), stop=(ci == DC - 1),
                  reads=[("xb", b), "ones_b"], writes=[("ps", 6)])
            S.add("tensor", "matmul", ps[5][0:1, 0:TT], ones_b[:], sqf[b][:],
                  start=(ci == 0), stop=(ci == DC - 1),
                  reads=[("sqf", b), "ones_b"], writes=[("ps", 5)])

    def rstd_rows(bank_unused, mean_row, rstd_row, tmp_row):
        S.add("vector", "tensor_scalar", rows[mean_row][:],
              ps[6][0:1, 0:TT], 1.0 / D, None, OP.mult,
              reads=[("ps", 6)], writes=[("row", mean_row)])
        S.add("vector", "tensor_scalar", rows[tmp_row][:],
              ps[5][0:1, 0:TT], 1.0 / D, None, OP.mult,
              reads=[("ps", 5)], writes=[("row", tmp_row)])
        S.add("vector", "tensor_tensor", rows[rstd_row][:], rows[mean_row][:],
              rows[mean_row][:], OP.mult, reads=[("row", mean_row)],
              writes=[("row", rstd_row)])
        S.add("vector", "tensor_tensor", rows[tmp_row][:], rows[tmp_row][:],
              rows[rstd_row][:], OP.subtract,
              reads=[("row", tmp_row), ("row", rstd_row)],
              writes=[("row", tmp_row)])
        S.add("vector", "tensor_scalar", rows[tmp_row][:], rows[tmp_row][:],
              EPS, None, OP.add, reads=[("row", tmp_row)],
              writes=[("row", tmp_row)])
        S.add("scalar", "activation", rows[rstd_row][:], rows[tmp_row][:],
              AF.Ln, reads=[("row", tmp_row)], writes=[("row", rstd_row)])
        S.add("scalar", "activation", rows[rstd_row][:], rows[rstd_row][:],
              AF.Exp, scale=-0.5, reads=[("row", rstd_row)],
              writes=[("row", rstd_row)])

    bcast_slot = [0]

    def bcast_row(row_key, dst, dst_key):
        sl = bcast_slot[0] % 8
        bcast_slot[0] += 1
        S.add("gpsimd", "dma_start", drows[sl].ap(), rows[row_key][:],
              dma=True, reads=[("row", row_key)], writes=[("drow", sl)])
        S.add("gpsimd", "dma_start", dst[:],
              drows[sl].broadcast_to([128, TT])[:, :], dma=True,
              reads=[("drow", sl)], writes=[dst_key])

    def tm_to_row(tm_ap, tm_keys, row_key, psn=7):
        for blk in range(NBLK):
            S.add("tensor", "matmul", ps[psn][0:1, blk * 128:(blk + 1) * 128],
                  tm_ap[:, blk:blk + 1], ident[:], start=True, stop=True,
                  reads=tm_keys + ["ident"], writes=[("ps", psn)])
        S.add("vector", "tensor_copy", rows[row_key][:], ps[psn][0:1, 0:TT],
              reads=[("ps", psn)], writes=[("row", row_key)])

    def amax_reduce(src_ap, src_keys, out_row, psn=7):
        for blk in range(NBLK):
            S.add("tensor", "transpose",
                  ps[psn][0:128, blk * 128:(blk + 1) * 128],
                  src_ap[:, blk * 128:(blk + 1) * 128], ident[:],
                  reads=src_keys + ["ident"], writes=[("ps", psn)])
        for blk in range(NBLK):
            S.add("vector", "tensor_reduce", amax_tm[:, blk:blk + 1],
                  ps[psn][0:128, blk * 128:(blk + 1) * 128], AX.X, OP.max,
                  reads=[("ps", psn)], writes=["amax_tm"])
        tm_to_row(amax_tm[:], ["amax_tm"], out_row, psn)

    def recip_row(dst, src, tmp):
        S.add("scalar", "activation", rows[tmp][:], rows[src][:], AF.Ln,
              reads=[("row", src)], writes=[("row", tmp)])
        S.add("scalar", "activation", rows[dst][:], rows[tmp][:], AF.Exp,
              scale=-1.0, reads=[("row", tmp)], writes=[("row", dst)])
        S.add("vector", "tensor_tensor", rows[tmp][:], rows[src][:],
              rows[dst][:], OP.mult, reads=[("row", src), ("row", dst)],
              writes=[("row", tmp)])
        S.add("vector", "tensor_scalar", rows[tmp][:], rows[tmp][:], -1.0, 2.0,
              OP.mult, OP.add, reads=[("row", tmp)], writes=[("row", tmp)])
        S.add("vector", "tensor_tensor", rows[dst][:], rows[dst][:],
              rows[tmp][:], OP.mult, reads=[("row", dst), ("row", tmp)],
              writes=[("row", dst)])

    S.add("sync", "dma_start", ident[:], dident, dma=True, writes=["ident"])
    S.add("vector", "memset", ones_f[:], 1.0, writes=["ones_f"])
    S.add("vector", "memset", ones_b[:], 1.0, writes=["ones_b"])
    S.add("vector", "memset", ones_r[:], 1.0, writes=["ones_r"])
    for ci in range(DC):
        S.add("gpsimd", "dma_start", x[:, ci, :], dx[ci], dma=True,
              writes=[("x", ci, tt) for tt in range(NTILE)])

    ast = contextlib.ExitStack()
    crow = sb("crow", [1, 4], F32, ast)
    dstg = [sb(f"dstg{i}", [128, D], F32, ast) for i in range(1)]
    wraw2 = [dstg[0], sb("wraw2_1", [128, 1024], F32, ast)]
    wtn2 = [sb(f"wtn2_{i}", [128, 1024], BF16, ast) for i in range(2)]
    negM2a = sb("negM2a", [128, 1], F32, ast)
    bca = {k: sb("bca_" + k, [128, TT], BF16, ast) for k in ("P", "Q")}
    rowbf = sb("rowbf", [1, TT], BF16, ast)
    bw_bf = sb("bw_bf", [128, DC, 128], BF16, ast)
    c1_bf = sb("c1_bf", [NS, D], BF16, ast)
    c2_bf = sb("c2_bf", [NS, D], BF16, ast)
    dw_bf = sb("dw_bf", [128, DC, D], BF16, ast)
    apar = sb("apar", [NS, 16], F32, ast)
    xq1 = sb("xq1", [128, DC, T], BF16, ast)
    ucx = sb("ucx", [NS, 2 * T], F32, ast)
    Ut = sb("Ut", [NS, 2 * T], F32, ast)
    bp = sb("bp", [NS, T], F32, ast)
    rbuf = sb("rbuf", [NS, T], F32, ast)
    h_bf = sb("h_bf", [NS, 2 * T], BF16, ast)

    S.add("sync", "dma_start", apar[:, 0:1], dlar, dma=True,
          writes=[("apar", 0)])
    S.add("sync", "dma_start", apar[:, 1:2], dlai, dma=True,
          writes=[("apar", 1)])
    for ci in range(DC):
        b = 0
        S.add("sync", "dma_start", dstg[b][:, 0:128], dbw[ci], dma=True,
              writes=[("dstg", b)])
        S.add("scalar", "copy", bw_bf[:, ci, :], dstg[b][:, 0:128],
              reads=[("dstg", b)], writes=["bw_bf"])
    for src, dst, key in ((dc1, c1_bf, "c1_bf"), (dc2, c2_bf, "c2_bf")):
        S.add("sync", "dma_start", dstg[0][0:NS, :], src, dma=True,
              writes=[("dstg", 0)])
        S.add("scalar", "copy", dst[:], dstg[0][0:NS, :],
              reads=[("dstg", 0)], writes=[key])
    for ci in range(DC):
        S.add("sync", "dma_start", dstg[0][:], ddw[ci], dma=True,
              writes=[("dstg", 0)])
        S.add("scalar", "copy", dw_bf[:, ci, :], dstg[0][:],
              reads=[("dstg", 0)], writes=[("dw_bf", ci)])

    def _shard_stats():
        for i in range(12):
            c, h = i // 2, i % 2
            S.add("scalar", "dma_start", dstg[0][:, 0:1024],
                  dsh[c, :, h * 1024:(h + 1) * 1024], dma=True,
                  writes=[("dstg", 0)])
            S.add("scalar", "activation", dstg[0][:, 0:1024], dstg[0][:, 0:1024],
                  AF.Abs, accum_out=wsum[:, i:i + 1],
                  reads=[("dstg", 0)], writes=[("dstg", 0), ("wsum", i)])
        for m in range(3):
            S.add("vector", "tensor_reduce", wsum[:, 16 + m:17 + m],
                  wsum[:, 4 * m:4 * m + 4], AX.X, OP.add,
                  reads=[("wsum", i) for i in range(4 * m, 4 * m + 4)],
                  writes=[("wsum", "t", m)])
            S.add("tensor", "matmul", ps[7][0:1, m:m + 1],
                  wsum[:, 16 + m:17 + m], ones_f[:], start=True, stop=True,
                  reads=[("wsum", "t", m), "ones_f"], writes=[("ps", 7)])
        S.add("vector", "memset", crow[:], 0.0, writes=["crow"])
        S.add("vector", "tensor_copy", crow[:, 0:3], ps[7][0:1, 0:3],
              reads=[("ps", 7), "crow"], writes=["crow"])
        S.add("scalar", "dma_start", dccin.ap(), crow[:], dma=True,
              reads=["crow"], writes=["dccin"])
        if cc_local:
            S.add("gpsimd", "dma_start", dccout.ap(), dccin.ap(), dma=True,
                  reads=["dccin"], writes=["dccout"])
        else:
            S.add("gpsimd", "collective_compute", "AllReduce", OP.add,
                  [[0, 1, 2, 3, 4, 5, 6, 7]], [dccin.ap()], [dccout.ap()],
                  cc=True, reads=["dccin"], writes=["dccout"])
    def _scale_chain():
        S.add("scalar", "dma_start", crow[:], dccout.ap(), dma=True,
              reads=["dccout"], writes=["crow"])
        for m in range(3):
            S.add("vector", "tensor_scalar", wrow[0:1, 0:1], crow[:, m:m + 1],
                  1.0 / WSIZE, 1e-5, OP.mult, OP.max,
                  reads=["crow"], writes=[("wrow", 0)])
            S.add("scalar", "activation", wrow[0:1, 1:2], wrow[0:1, 0:1], AF.Ln,
                  reads=[("wrow", 0)], writes=[("wrow", 1)])
            S.add("scalar", "activation", wrow[0:1, 2:3], wrow[0:1, 1:2],
                  AF.Exp, scale=-1.0, reads=[("wrow", 1)], writes=[("wrow", 2)])
            S.add("vector", "tensor_tensor", wrow[0:1, 3:4], wrow[0:1, 0:1],
                  wrow[0:1, 2:3], OP.mult, reads=[("wrow", 0), ("wrow", 2)],
                  writes=[("wrow", 3)])
            S.add("vector", "tensor_scalar", wrow[0:1, 3:4], wrow[0:1, 3:4],
                  -1.0, 2.0, OP.mult, OP.add, reads=[("wrow", 3)],
                  writes=[("wrow", 3)])
            S.add("vector", "tensor_tensor", wrow[0:1, 2:3], wrow[0:1, 2:3],
                  wrow[0:1, 3:4], OP.mult, reads=[("wrow", 2), ("wrow", 3)],
                  writes=[("wrow", 2)])
            S.add("tensor", "matmul", ps[7][:, 0:1], ones_r[:], wrow[0:1, 0:1],
                  start=True, stop=True, reads=[("wrow", 0), "ones_r"],
                  writes=[("ps", 7)])
            S.add("tensor", "matmul", ps[7][:, 1:2], ones_r[:], wrow[0:1, 2:3],
                  start=True, stop=True, reads=[("wrow", 2), "ones_r"],
                  writes=[("ps", 7)])
            S.add("vector", "tensor_copy", mg128[:, m:m + 1], ps[7][:, 0:1],
                  reads=[("ps", 7)], writes=[("mg128", m)])
            S.add("vector", "tensor_copy", s128[:, m:m + 1], ps[7][:, 1:2],
                  reads=[("ps", 7)], writes=[("s128", m)])


    def a_params():
        A = apar

        def k(j):
            return ("apar", j)

        def ts(dst, src, s1, s2, o1, o2=None, extra=()):
            S.add("vector", "tensor_scalar", A[:, dst:dst + 1],
                  A[:, src:src + 1], s1, s2, o1, *([o2] if o2 else []),
                  reads=[k(src)] + list(extra), writes=[k(dst)])

        def tt(dst, a1, a2, op):
            S.add("vector", "tensor_tensor", A[:, dst:dst + 1],
                  A[:, a1:a1 + 1], A[:, a2:a2 + 1], op,
                  reads=[k(a1), k(a2)], writes=[k(dst)])

        def act(dst, src, func, **kw):
            S.add("scalar", "activation", A[:, dst:dst + 1],
                  A[:, src:src + 1], func, reads=[k(src)], writes=[k(dst)],
                  **kw)

        act(2, 0, AF.Exp)
        ts(3, 2, -0.5, None, OP.mult)
        ts(4, 1, 0.5, None, OP.mult)
        ts(7, 3, -1.0, 1.0, OP.mult, OP.add)
        tt(5, 7, 7, OP.mult)
        tt(7, 4, 4, OP.mult)
        tt(5, 5, 7, OP.add)
        act(14, 5, AF.Ln)
        act(6, 14, AF.Exp, scale=-1.0)
        tt(14, 5, 6, OP.mult)
        ts(14, 14, -1.0, 2.0, OP.mult, OP.add)
        tt(6, 6, 14, OP.mult)
        tt(8, 3, 3, OP.mult)
        tt(7, 4, 4, OP.mult)
        tt(8, 8, 7, OP.add)
        ts(8, 8, -1.0, 1.0, OP.mult, OP.add)
        tt(8, 8, 6, OP.mult)
        ts(9, 4, 2.0, None, OP.mult)
        tt(9, 9, 6, OP.mult)
        tt(10, 8, 8, OP.mult)
        tt(7, 9, 9, OP.mult)
        tt(10, 10, 7, OP.add)
        ts(7, 10, 1e-38, None, OP.max)
        act(14, 7, AF.Ln)
        act(11, 14, AF.Exp, scale=0.5)
        ts(7, 11, 1e-30, None, OP.max)
        act(14, 7, AF.Ln)
        act(7, 14, AF.Exp, scale=-1.0)
        tt(12, 8, 7, OP.mult)
        tt(13, 9, 7, OP.mult)
        ts(14, 11, -1e-30, 0.0, OP.add, OP.max)
        S.add("scalar", "sign", A[:, 14:15], A[:, 14:15],
              reads=[k(14)], writes=[k(14)])
        ts(7, 14, -1.0, 1.0, OP.mult, OP.add)
        tt(12, 12, 14, OP.mult)
        tt(12, 12, 7, OP.add)
        tt(13, 13, 14, OP.mult)
        tt(7, 12, 12, OP.mult)
        tt(14, 13, 13, OP.mult)
        tt(7, 7, 14, OP.add)
        ts(7, 7, -0.5, 1.5, OP.mult, OP.add)
        tt(12, 12, 7, OP.mult)
        tt(13, 13, 7, OP.mult)

    def u_table():
        A = apar
        S.add("vector", "memset", Ut[0:NS, 0:2 * T], 0.0, writes=[("Ut", "z")])
        S.add("vector", "memset", Ut[0:NS, 0:1], 1.0,
              reads=[("Ut", "z")], writes=[("Ut", 0)])
        S.add("vector", "memset", rbuf[:], 1.0, writes=["rbuf"])
        S.add("vector", "tensor_scalar", rbuf[:], rbuf[:], apar[:, 11:12],
              None, OP.mult, reads=["rbuf", ("apar", 11)], writes=["rbuf"])
        L = 1
        while L < T:
            upto = min(L, T - L)
            seg = int(np.log2(L)) + 1
            rdU = [("Ut", j) for j in range(seg)] + [("Ut", "z")]
            kP = [("apar", 12), ("apar", 13)]
            S.add("vector", "tensor_scalar", bp[0:NS, 0:upto],
                  Ut[0:NS, 0:upto], A[:, 12:13], None, OP.mult,
                  reads=rdU + kP, writes=[("bp", "t")])
            S.add("vector", "tensor_scalar", A[:, 15:16], A[:, 13:14], -1.0,
                  None, OP.mult, reads=[("apar", 13)], writes=[("apar", 15)])
            S.add("vector", "scalar_tensor_tensor", Ut[0:NS, L:L + upto],
                  Ut[0:NS, T:T + upto], A[:, 15:16], bp[0:NS, 0:upto],
                  OP.mult, OP.add,
                  reads=rdU + kP + [("apar", 15), ("bp", "t")],
                  writes=[("Ut", seg)])
            S.add("vector", "tensor_scalar", bp[0:NS, 0:upto],
                  Ut[0:NS, T:T + upto], A[:, 12:13], None, OP.mult,
                  reads=rdU + kP, writes=[("bp", "t")])
            S.add("vector", "scalar_tensor_tensor",
                  Ut[0:NS, T + L:T + L + upto], Ut[0:NS, 0:upto],
                  A[:, 13:14], bp[0:NS, 0:upto], OP.mult, OP.add,
                  reads=rdU + kP + [("bp", "t")], writes=[("Ut", seg)])
            S.add("vector", "tensor_tensor", A[:, 14:15], A[:, 12:13],
                  A[:, 12:13], OP.mult, reads=[("apar", 12)],
                  writes=[("apar", 14)])
            S.add("vector", "tensor_tensor", A[:, 15:16], A[:, 13:14],
                  A[:, 13:14], OP.mult, reads=[("apar", 13)],
                  writes=[("apar", 15)])
            S.add("vector", "tensor_tensor", A[:, 13:14], A[:, 12:13],
                  A[:, 13:14], OP.mult, reads=[("apar", 12), ("apar", 13)],
                  writes=[("apar", 13)])
            S.add("vector", "tensor_scalar", A[:, 13:14], A[:, 13:14], 2.0,
                  None, OP.mult, reads=[("apar", 13)], writes=[("apar", 13)])
            S.add("vector", "tensor_tensor", A[:, 12:13], A[:, 14:15],
                  A[:, 15:16], OP.subtract,
                  reads=[("apar", 14), ("apar", 15)], writes=[("apar", 12)])
            L *= 2

    UKEYS = [("Ut", j) for j in range(12)] + [("Ut", "z")]

    def ln1_tile(tile):
        t0 = tile * TT
        ln_stats(tile)
        rstd_rows(6, "A", "B", "C")
        S.add("vector", "tensor_tensor", rows["C"][:], rows["A"][:],
              rows["B"][:], OP.mult, reads=[("row", "A"), ("row", "B")],
              writes=[("row", "C")])
        for rk, bk, sl in (("B", "P", 0), ("C", "Q", 1)):
            S.add("vector", "tensor_copy", rowbf[:], rows[rk][:],
                  reads=[("row", rk)], writes=["rowbf"])
            S.add("gpsimd", "dma_start", drowb[sl].ap(), rowbf[:], dma=True,
                  reads=["rowbf"], writes=[("drowb", sl)])
            S.add("gpsimd", "dma_start", bca[bk][:],
                  drowb[sl].broadcast_to([128, TT])[:, :], dma=True,
                  reads=[("drowb", sl)], writes=[("bca", bk)])
        for ci in range(DC):
            b = ci % 2
            S.add("vector", "tensor_tensor", sqf[b][:],
                  x[:, ci, t0:t0 + TT], bca["P"][:], OP.mult,
                  reads=[("x", ci, tile), ("bca", "P")], writes=[("sqf", b)])
            S.add("vector", "tensor_tensor", xq1[:, ci, t0:t0 + TT],
                  sqf[b][:], bca["Q"][:], OP.subtract,
                  reads=[("sqf", b), ("bca", "Q")],
                  writes=[("xq1", ci, tile)])

    def bu_tile(tile):
        t0 = tile * TT
        for half in (0, 1):
            pi = half
            for ci in range(DC):
                S.add("tensor", "matmul", ps[pi][0:NS, 0:TT],
                      bw_bf[:, ci, half * NS:(half + 1) * NS],
                      xq1[:, ci, t0:t0 + TT],
                      start=(ci == 0), stop=(ci == DC - 1),
                      reads=[("xq1", ci, tile), "bw_bf"], writes=[("ps", pi)])
            S.add("vector", "tensor_copy",
                  ucx[0:NS, half * T + t0: half * T + t0 + TT],
                  ps[pi][0:NS, 0:TT], reads=[("ps", pi)],
                  writes=[("ucx", half, tile)])

    def rotate_scan():
        UK = UKEYS
        BK = [("ucx", hh, tt) for hh in (0, 1) for tt in range(NTILE)]
        S.add("vector", "tensor_tensor", h_bf[0:NS, 0:T], Ut[0:NS, T:2 * T],
              ucx[0:NS, 0:T], OP.mult, reads=UK + BK, writes=[("h", "tmpA")])
        S.add("vector", "tensor_tensor", ucx[0:NS, 0:T], ucx[0:NS, 0:T],
              Ut[0:NS, 0:T], OP.mult, reads=UK + BK, writes=[("ucx", "re1")])
        S.add("vector", "tensor_tensor", bp[0:NS, 0:T], Ut[0:NS, T:2 * T],
              ucx[0:NS, T:2 * T], OP.mult, reads=UK + BK, writes=[("bp", "t")])
        S.add("vector", "tensor_tensor", ucx[0:NS, 0:T], ucx[0:NS, 0:T],
              bp[0:NS, 0:T], OP.add,
              reads=[("ucx", "re1"), ("bp", "t")], writes=[("ucx", "re")])
        S.add("vector", "tensor_tensor", ucx[0:NS, T:2 * T],
              ucx[0:NS, T:2 * T], Ut[0:NS, 0:T], OP.mult,
              reads=UK + BK, writes=[("ucx", "im1")])
        S.add("vector", "tensor_tensor", ucx[0:NS, T:2 * T],
              ucx[0:NS, T:2 * T], h_bf[0:NS, 0:T], OP.subtract,
              reads=[("ucx", "im1"), ("h", "tmpA")], writes=[("ucx", "im")])
        S.add("vector", "tensor_tensor_scan", ucx[0:NS, 0:T], rbuf[:],
              ucx[0:NS, 0:T], 0.0, OP.mult, OP.add,
              reads=["rbuf", ("ucx", "re")], writes=[("ucx", "gre")])
        S.add("vector", "tensor_tensor_scan", ucx[0:NS, T:2 * T], rbuf[:],
              ucx[0:NS, T:2 * T], 0.0, OP.mult, OP.add,
              reads=["rbuf", ("ucx", "im")], writes=[("ucx", "gim")])
        S.add("vector", "tensor_tensor", bp[0:NS, 0:T], Ut[0:NS, T:2 * T],
              ucx[0:NS, T:2 * T], OP.mult, reads=UK + [("ucx", "gim")],
              writes=[("bp", "t2")])
        S.add("vector", "tensor_tensor", h_bf[0:NS, T:2 * T],
              Ut[0:NS, T:2 * T], ucx[0:NS, 0:T], OP.mult,
              reads=UK + [("ucx", "gre")], writes=[("h", "tmpX")])
        S.add("vector", "tensor_tensor", ucx[0:NS, 0:T], ucx[0:NS, 0:T],
              Ut[0:NS, 0:T], OP.mult, reads=UK + [("ucx", "gre")],
              writes=[("ucx", "p1")])
        S.add("vector", "tensor_tensor", ucx[0:NS, T:2 * T],
              ucx[0:NS, T:2 * T], Ut[0:NS, 0:T], OP.mult,
              reads=UK + [("ucx", "gim")], writes=[("ucx", "p3")])
        S.add("vector", "tensor_tensor", h_bf[0:NS, 0:T], ucx[0:NS, 0:T],
              bp[0:NS, 0:T], OP.subtract,
              reads=[("ucx", "p1"), ("bp", "t2")], writes=[("h", 0)])
        S.add("vector", "tensor_tensor", h_bf[0:NS, T:2 * T],
              ucx[0:NS, T:2 * T], h_bf[0:NS, T:2 * T], OP.add,
              reads=[("ucx", "p3"), ("h", "tmpX")], writes=[("h", 1)])

    def attn_out_tile(tile):
        t0 = tile * TT
        for oc in range(DC):
            pi = 2 + (oc % 4)
            S.add("tensor", "matmul", ps[pi][:, 0:TT],
                  c1_bf[0:NS, oc * 128:(oc + 1) * 128],
                  h_bf[0:NS, t0:t0 + TT], start=True, stop=False,
                  reads=[("h", 0), "c1_bf"], writes=[("ps", pi)])
            S.add("tensor", "matmul", ps[pi][:, 0:TT],
                  c2_bf[0:NS, oc * 128:(oc + 1) * 128],
                  h_bf[0:NS, T + t0:T + t0 + TT], start=False, stop=False,
                  reads=[("h", 1), "c2_bf"], writes=[("ps", pi)])
            for ci in range(DC):
                S.add("tensor", "matmul", ps[pi][:, 0:TT],
                      dw_bf[:, ci, oc * 128:(oc + 1) * 128],
                      xq1[:, ci, t0:t0 + TT],
                      start=False, stop=(ci == DC - 1),
                      reads=[("xq1", ci, tile), ("dw_bf", ci)],
                      writes=[("ps", pi)])
            S.add("vector", "tensor_tensor", x[:, oc, t0:t0 + TT],
                  x[:, oc, t0:t0 + TT], ps[pi][:, 0:TT], OP.add,
                  reads=[("x", oc, tile), ("ps", pi)],
                  writes=[("x", oc, tile)])

    def dump(nm, src_ap, read_keys):
        if debug:
            S.add("gpsimd", "dma_start", dbg[nm], src_ap, dma=True,
                  reads=read_keys, writes=[("dbg", nm)])

    def quant_all():
        S.add("vector", "memset", negM2a[:], -MAGIC, writes=["negM2a"])
        i = 0
        for k in range(3):
            for j in range(2):
                for h in range(2):
                    b = i % 2
                    kb = ("dstg", 0) if b == 0 else ("wraw2", 1)
                    S.add("sync", "dma_start", wraw2[b][:],
                          dsh[2 * k + j, :, 1024 * h:1024 * h + 1024],
                          dma=True, writes=[kb])
                    S.add("vector", "tensor_scalar", wraw2[b][:],
                          wraw2[b][:], s128[:, k:k + 1], MAGIC,
                          OP.mult, OP.add,
                          reads=[kb, ("s128", k)], writes=[kb])
                    S.add("vector", "tensor_scalar", wraw2[b][:],
                          wraw2[b][:], MAGIC + 1.0, MAGIC - 1.0,
                          OP.min, OP.max, reads=[kb], writes=[kb])
                    S.add("scalar", "activation", wtn2[b][:], wraw2[b][:],
                          AF.Identity, bias=negM2a[:, 0:1],
                          reads=[kb, "negM2a"],
                          writes=[("wtn2", b)])
                    S.add("gpsimd", "dma_start",
                          tmine[k, j, :, 1024 * h:1024 * h + 1024],
                          wtn2[b][:], dma=True,
                          reads=[("wtn2", b)], writes=[("mine", i)])
                    i += 1

    def gather_cc():
        if cc_local:
            S.add("gpsimd", "dma_start", tall[0:3], tmine, dma=True,
                  reads=[("mine", k) for k in range(12)], writes=["ternall"])
        else:
            S.add("gpsimd", "collective_compute", "AllGather", OP.bypass,
                  [[0, 1, 2, 3, 4, 5, 6, 7]], [tmine], [tall],
                  cc=True, reads=[("mine", k) for k in range(12)],
                  writes=["ternall"])

    _shard_stats()
    a_params()
    u_table()
    ln1_tile(0)
    ln1_tile(1)
    bu_tile(0)
    ln1_tile(2)
    bu_tile(1)
    _scale_chain()
    ln1_tile(3)
    bu_tile(2)
    quant_all()
    bu_tile(3)
    gather_cc()
    rotate_scan()
    for tile in range(NTILE):
        attn_out_tile(tile)
    dump("d_xq1", xq1[:], [("xq1", ci, tt) for ci in range(DC)
                           for tt in range(NTILE)])
    dump("d_Ut", Ut[:], UKEYS)
    dump("d_rbuf", rbuf[:], ["rbuf"])
    dump("d_h", h_bf[:], [("h", 0), ("h", 1)])
    dump("d_apar", apar[:], [("apar", j) for j in range(16)])
    dump("d_x1", x[:], [("x", ci, tt) for ci in range(DC)
                        for tt in range(NTILE)])

    S.fence()
    ast.close()

    S.fence()

    fst = contextlib.ExitStack()
    wgs = [sb(f"wgs{i}", [128, DC, 256], BF16, fst) for i in range(2)]
    wus = [sb(f"wus{i}", [128, DC, 256], BF16, fst) for i in range(2)]
    wdk = [sb(f"wdk{i}", [128, D], BF16, fst) for i in range(2)]
    q1 = sb("q1", [128, DC, TT], BF16, fst)
    gates = sb("gates", [128, IC, TT], F32, fst)
    guq = gates.bitcast(BF16)
    bc = {k: sb("bc_" + k, [128, TT], F32, fst) for k in ("P", "Q", "S")}
    gsc = [sb(f"gsc{i}", [128, TT], F32, fst) for i in range(2)]
    absb = [sb(f"absb{i}", [128, TT], F32, fst) for i in range(1)]
    cmax = sb("cmax", [128, TT], F32, fst)
    amax_tm = sb("amax_tm", [128, NBLK], F32, fst)
    def s1a(tile):
        t0 = tile * TT
        ln_stats(tile)
        rstd_rows(6, "A", "B", "C")
        bcast_row("A", bc["P"], ("bc", "P"))
        S.add("vector", "memset", cmax[:], 0.0, writes=["cmax"])
        for ci in range(DC):
            b = ci % 2
            S.add("vector", "tensor_tensor", gsc[b][:], x[:, ci, t0:t0 + TT],
                  bc["P"][:], OP.subtract,
                  reads=[("x", ci, tile), ("bc", "P")], writes=[("gsc", b)])
            S.add("scalar", "activation", absb[0][:], gsc[b][:], AF.Abs,
                  reads=[("gsc", b)], writes=[("absb", 0)])
            S.add("vector", "tensor_tensor", cmax[:], cmax[:], absb[0][:],
                  OP.max, reads=["cmax", ("absb", 0)], writes=["cmax"])
        amax_reduce(cmax[:], ["cmax"], "C")

    def s1b1(tile):
        S.add("vector", "tensor_tensor", rows["C"][:], rows["C"][:],
              rows["B"][:], OP.mult, reads=[("row", "C"), ("row", "B")],
              writes=[("row", "C")])
        S.add("vector", "tensor_scalar", rows["C"][:], rows["C"][:], 1e-5,
              None, OP.max, reads=[("row", "C")], writes=[("row", "C")])
        recip_row("D", "C", "F")
        S.add("vector", "tensor_scalar", rows["D"][:], rows["D"][:], 127.0,
              None, OP.mult, reads=[("row", "D")], writes=[("row", "D")])
        S.add("vector", "tensor_tensor", rows["D"][:], rows["D"][:],
              rows["B"][:], OP.mult, reads=[("row", "D"), ("row", "B")],
              writes=[("row", "D")])
        S.add("vector", "tensor_scalar", rows["A"][:], rows["C"][:],
              1.0 / 127.0, None, OP.mult, reads=[("row", "C")],
              writes=[("row", "A")])

    def s1b2(tile):
        t0 = tile * TT
        bcast_row("D", bc["Q"], ("bc", "Q"))
        for ci in range(DC):
            b = ci % 2
            S.add("vector", "tensor_tensor", gsc[b][:], x[:, ci, t0:t0 + TT],
                  bc["P"][:], OP.subtract,
                  reads=[("x", ci, tile), ("bc", "P")], writes=[("gsc", b)])
            S.add("vector", "tensor_tensor", gsc[b][:], gsc[b][:],
                  bc["Q"][:], OP.mult,
                  reads=[("gsc", b), ("bc", "Q")], writes=[("gsc", b)])
            rne_round(q1[:, ci, :], gsc[b][:], reads=[("gsc", b)],
                      writes=[("q1", ci)])
        S.add("vector", "tensor_scalar", rows["F"][:], rows["A"][:],
              mg128[0:1, 0:1], None, OP.mult,
              reads=[("row", "A"), ("mg128", 0)], writes=[("row", "F")])
        bcast_row("F", bc["P"], ("bc", "P"))
        S.add("vector", "tensor_scalar", rows["C"][:], rows["A"][:],
              mg128[0:1, 1:2], None, OP.mult,
              reads=[("row", "A"), ("mg128", 1)], writes=[("row", "C")])

    def s2(tile):
        S.add("vector", "memset", cmax[:], 0.0, writes=["cmax"])
        for oc in range(IC):
            ob, j = oc // 2, oc % 2
            bb = ob % 2
            if j == 0:
                S.add("sync", "dma_start", wgs[bb][:],
                      tall[3 * (ob // 2), ob % 2], dma=True,
                      reads=["ternall"], writes=[("wgs", bb)])
                S.add("sync", "dma_start", wus[bb][:],
                      tall[3 * (ob // 2) + 1, ob % 2], dma=True,
                      reads=["ternall"], writes=[("wus", bb)])
            pg = oc % 3
            pu = 3 + oc % 3
            for ci in range(DC):
                S.add("tensor", "matmul", ps[pg][:, 0:TT],
                      wgs[bb][:, ci, j * 128:(j + 1) * 128], q1[:, ci, :],
                      start=(ci == 0), stop=(ci == DC - 1),
                      reads=[("wgs", bb), ("q1", ci)], writes=[("ps", pg)])
            S.add("vector", "tensor_tensor", gsc[oc % 2][:], ps[pg][:, 0:TT],
                  bc["P"][:], OP.mult, reads=[("ps", pg), ("bc", "P")],
                  writes=[("gsc", oc % 2)])
            S.add("scalar", "activation", gates[:, oc, :], gsc[oc % 2][:],
                  AF.Silu, reads=[("gsc", oc % 2)],
                  writes=[("gates", oc), ("guq", oc)])
            for ci in range(DC):
                S.add("tensor", "matmul", ps[pu][:, 0:TT],
                      wus[bb][:, ci, j * 128:(j + 1) * 128], q1[:, ci, :],
                      start=(ci == 0), stop=(ci == DC - 1),
                      reads=[("wus", bb), ("q1", ci)], writes=[("ps", pu)])
            S.add("vector", "tensor_tensor", gates[:, oc, :],
                  gates[:, oc, :], ps[pu][:, 0:TT], OP.mult,
                  reads=[("gates", oc), ("ps", pu)],
                  writes=[("gates", oc), ("guq", oc)])
            S.add("scalar", "activation", absb[0][:], gates[:, oc, :],
                  AF.Abs, reads=[("gates", oc)], writes=[("absb", 0)])
            S.add("vector", "tensor_tensor", cmax[:], cmax[:],
                  absb[0][:], OP.max,
                  reads=["cmax", ("absb", 0)], writes=["cmax"])

    def s3(tile):
        amax_reduce(cmax[:], ["cmax"], "B")
        S.add("vector", "tensor_tensor", rows["E"][:], rows["C"][:],
              rows["B"][:], OP.mult, reads=[("row", "C"), ("row", "B")],
              writes=[("row", "E")])
        S.add("vector", "tensor_scalar", rows["E"][:], rows["E"][:], 1e-5,
              None, OP.max, reads=[("row", "E")], writes=[("row", "E")])
        recip_row("D", "E", "F")
        S.add("vector", "tensor_scalar", rows["D"][:], rows["D"][:], 127.0,
              None, OP.mult, reads=[("row", "D")], writes=[("row", "D")])
        S.add("vector", "tensor_tensor", rows["D"][:], rows["D"][:],
              rows["C"][:], OP.mult, reads=[("row", "D"), ("row", "C")],
              writes=[("row", "D")])
        S.add("vector", "tensor_scalar", rows["E"][:], rows["E"][:],
              mg128[0:1, 2:3], 1.0 / 127.0, OP.mult, OP.mult,
              reads=[("row", "E"), ("mg128", 2)], writes=[("row", "E")])
        bcast_row("D", bc["Q"], ("bc", "Q"))
        bcast_row("E", bc["S"], ("bc", "S"))

    def s4(tile):
        for oc in range(IC):
            b = oc % 2
            S.add("vector", "tensor_tensor", gsc[b][:], gates[:, oc, :],
                  bc["Q"][:], OP.mult, reads=[("gates", oc), ("bc", "Q")],
                  writes=[("gsc", b)])
            rne_round(guq[:, oc, 0:TT], gsc[b][:], reads=[("gsc", b)],
                      writes=[("guq", oc), ("gates", oc)])

    def s5(tile):
        t0 = tile * TT
        for kc in range(IC):
            bb = kc % 2
            hd = (kc % 2) * 1024
            S.add("sync", "dma_start", wdk[bb][:],
                  tall[3 * (kc // 4) + 2, (kc % 4) // 2, :, hd:hd + 1024],
                  dma=True, reads=["ternall"], writes=[("wdk", bb)])
            for od in range(DC):
                S.add("tensor", "matmul", ps[od][:, 0:TT],
                      wdk[bb][:, od * 128:(od + 1) * 128], guq[:, kc, 0:TT],
                      start=(kc == 0), stop=(kc == IC - 1),
                      reads=[("wdk", bb), ("guq", kc)], writes=[("ps", od)])
        for od in range(DC):
            S.add("vector", "tensor_tensor", gsc[od % 2][:],
                  ps[od][:, 0:TT], bc["S"][:], OP.mult,
                  reads=[("ps", od), ("bc", "S")], writes=[("gsc", od % 2)])
            S.add("vector", "tensor_tensor", x[:, od, t0:t0 + TT],
                  x[:, od, t0:t0 + TT], gsc[od % 2][:], OP.add,
                  reads=[("x", od, tile), ("gsc", od % 2)],
                  writes=[("x", od, tile)])
            S.add("gpsimd", "dma_start", dy[od, :, t0:t0 + TT],
                  x[:, od, t0:t0 + TT], dma=True,
                  reads=[("x", od, tile)], writes=[("y", od, tile)])

    s1a(0)
    s1b1(0)
    s1b2(0)
    for tile in range(NTILE):
        s2(tile)
        s3(tile)
        if tile + 1 < NTILE:
            s1a(tile + 1)
        s4(tile)
        if tile + 1 < NTILE:
            s1b1(tile + 1)
            s1b2(tile + 1)
        s5(tile)
        if debug and tile == 0:
            dump("d_q1", q1[:], [("q1", ci) for ci in range(DC)])
            dump("d_gates", gates[:], [("gates", oc) for oc in range(IC)])
            dump("d_guq", guq[:, :, 0:TT], [("guq", oc) for oc in range(IC)])
            dump("d_rows", rows["A"][:], [("row", "A")])

    S.finalize()

    with nc.Block() as blk:
        @blk.sync
        def _(eng):
            S.replay("sync", eng, sems)
            for key, val in S.final_d.items():
                eng.wait_ge(sems[("d",) + key], val)
            for e in ENGS:
                if S.final_c[e]:
                    eng.wait_ge(sems[("c", e)], S.final_c[e])

        @blk.scalar
        def _(eng):
            S.replay("scalar", eng, sems)

        @blk.vector
        def _(eng):
            S.replay("vector", eng, sems)

        @blk.gpsimd
        def _(eng):
            S.replay("gpsimd", eng, sems)

        @blk.tensor
        def _(eng):
            S.replay("tensor", eng, sems)

    return nc, stack, fst


def kernel(L0_attn_u, L0_attn_log_A_real, L0_attn_log_A_imag, L0_attn_B_w,
           L0_attn_C_w, L0_attn_D_w, L0_ffn_w_gate, L0_ffn_w_up,
           L0_ffn_w_down):
    u = np.ascontiguousarray(L0_attn_u, dtype=np.float32)
    lar = np.ascontiguousarray(L0_attn_log_A_real, np.float32).reshape(NS, 1)
    lai = np.ascontiguousarray(L0_attn_log_A_imag, np.float32).reshape(NS, 1)
    bwT = np.ascontiguousarray(np.asarray(L0_attn_B_w, np.float32).T
                               ).reshape(DC, 128, 128)
    c1T = np.ascontiguousarray(np.asarray(L0_attn_C_w, np.float32)[:, :NS].T)
    c2T = np.ascontiguousarray(np.asarray(L0_attn_C_w, np.float32)[:, NS:].T)
    dwT = np.ascontiguousarray(np.asarray(L0_attn_D_w, np.float32).T
                               ).reshape(DC, 128, D)
    def _wg_layout(w):
        wt = np.asarray(w, np.float32).T.reshape(DC, 128, 16, 256)
        return np.ascontiguousarray(wt.transpose(2, 1, 0, 3))

    wgT = _wg_layout(L0_ffn_w_gate)
    wuT = _wg_layout(L0_ffn_w_up)
    wdT = np.ascontiguousarray(np.asarray(L0_ffn_w_down, np.float32).T
                               ).reshape(IC, 128, D)
    ident = np.eye(128, dtype=np.float32)

    nc, stack, fst = build_program()
    in_maps = []
    for c in range(NCORES):
        xc = np.ascontiguousarray(u[c].T).reshape(DC, 128, T)
        shard = np.stack([
            wgT[2 * c].reshape(128, 2048),
            wgT[2 * c + 1].reshape(128, 2048),
            wuT[2 * c].reshape(128, 2048),
            wuT[2 * c + 1].reshape(128, 2048),
            np.concatenate([wdT[4 * c], wdT[4 * c + 1]], axis=1),
            np.concatenate([wdT[4 * c + 2], wdT[4 * c + 3]], axis=1),
        ])
        in_maps.append({
            "x_in": xc, "lar": lar, "lai": lai, "bwT": bwT, "c1T": c1T,
            "c2T": c2T, "dwT": dwT, "wgT": wgT, "wuT": wuT, "wdT": wdT,
            "ident": ident, "wshard": np.ascontiguousarray(shard),
        })
    res = run_bass_kernel_spmd(nc, in_maps, core_ids=list(range(NCORES)))
    fst.close()
    stack.close()
    out = np.empty((NCORES, T, D), np.float32)
    for c in range(NCORES):
        yc = np.asarray(res.results[c]["y"])
        out[c] = yc.reshape(D, T).T
    return out
